# version 16
# baseline (speedup 1.0000x reference)
import contextlib
import math
import numpy as np

import concourse.bass as bass
import concourse.mybir as mybir
from concourse.bass_utils import run_bass_kernel_spmd

F32 = mybir.dt.float32
BF16 = mybir.dt.bfloat16
U8 = mybir.dt.uint8
AF = mybir.ActivationFunctionType
ALU = mybir.AluOpType
AX = mybir.AxisListType

T = 2048
D = 1024
NCH = 8
NE = 8
NK = 8
NJ = 4
TW = 512
DH = 32
RSQ_DH = 1.0 / math.sqrt(32.0)
EPS = 1e-5
NEGBIG = -1.0e30
THRFLOOR = -1.0e29
NEGMASK = -1.0e9

PHASES = ("ln1", "gate1", "ffn1", "ln2", "attn", "ln3", "gate2", "ffn2")
DUMP = ()
DEBUG_GATE = False
ATTN_SB = 4
COARSE_DEPS = False
ATTN_MSG = True

ENGINES = ("tensor", "vector", "scalar", "sync", "gpsimd")


class Prog:

    def __init__(self):
        self.steps = []
        self.counts = {}
        self.dma_sems = set()

    def sem(self, name, dma=False):
        self.counts[name] = 0
        if dma:
            self.dma_sems.add(name)

    def emit(self, engine, fn, waits=(), inc=None):
        w = [t for t in waits if t is not None]
        tok = None
        if inc is not None:
            n = 16 if inc in self.dma_sems else 1
            self.counts[inc] += n
            tok = (inc, self.counts[inc])
        self.steps.append((engine, fn, w, inc, tok))
        return tok

    def build(self, nc, sems):
        with nc.Block() as block:
            for eng_name in ENGINES:
                def stream(eng, eng_name=eng_name):
                    own = {s: 0 for s in self.counts}
                    waited = {s: 0 for s in self.counts}
                    skip_own = False
                    for engine, fn, waits, inc, tok in self.steps:
                        if engine != eng_name:
                            continue
                        for s, v in waits:
                            if skip_own and s not in self.dma_sems and own[s] >= v:
                                continue
                            if waited[s] >= v:
                                continue
                            eng.wait_ge(sems[s], v)
                            waited[s] = v
                        instr = fn(eng)
                        if inc is not None:
                            instr.then_inc(sems[inc], 16 if inc in self.dma_sems else 1)
                            own[inc] = tok[1]
                getattr(block, eng_name)(stream)


class Builder:
    def __init__(self, nc, es):
        self.nc = nc
        self.es = es
        self.p = Prog()
        for s in ("PE", "VE", "SC", "GP"):
            self.p.sem(s)
        for s in ("ST", "Xgb", "Xlm", "Xls", "Wq", "Wk", "Wg",
                  "W1a", "W1b", "W3a", "W3b", "W2a", "W2b"):
            self.p.sem(s, dma=True)
        for c in range(NCH):
            self.p.sem(f"Xx{c}", dma=True)
        self._declare_dram()
        self._declare_sbuf()
        self._declare_psum()
        self.bank_free = [None] * 8
        self.w1_free = [None, None]
        self.w3_free = [None, None]
        self.w2_free = [None, None]
        self.s1_free = [None] * NJ
        self.x3_free = [None] * NJ
        self.slot_free = [None, None]
        self.h_free = None
        self.crep_free = None
        self.sq_free = None
        self.sim_free = None
        self.xcast_free = None
        self.attnT_free = None
        self.btmp_free = None
        self.gelo_free = None

    def _declare_dram(self):
        nc = self.nc
        dt = nc.dram_tensor
        self.d_x = dt("x", [128, NCH, T], F32, kind="ExternalInput").ap()
        self.d_w1 = dt("w1", [NE, 2, NK, 128, NCH, 128], F32, kind="ExternalInput").ap()
        self.d_w3 = dt("w3", [NE, 2, NK, 128, NCH, 128], F32, kind="ExternalInput").ap()
        self.d_w2 = dt("w2", [NE, 2, 4, 128, NK, 256], F32, kind="ExternalInput").ap()
        self.d_wq = dt("wq", [128, NCH, DH], F32, kind="ExternalInput").ap()
        self.d_wk = dt("wk", [128, NCH, DH], F32, kind="ExternalInput").ap()
        self.d_gw = dt("gw", [128, NCH, NE], F32, kind="ExternalInput").ap()
        self.d_gb = dt("gb", [128, 16], F32, kind="ExternalInput").ap()
        self.d_lm = dt("lm", [1, 1], F32, kind="ExternalInput").ap()
        self.d_ls = dt("ls", [1, 1], F32, kind="ExternalInput").ap()
        self.d_y = dt("y", [128, NCH, T], F32, kind="ExternalOutput").ap()
        self.d_dump = {}
        for name in DUMP:
            shp, dtp = {
                "xn": ([128, NCH * T], BF16), "c_em": ([8, T], F32),
                "x_fm": ([128, NCH * T], F32), "simb": ([128, T], F32),
                "scorem": ([128, T], F32), "attnb": ([128, T], BF16),
                "h_half": ([128, NK * T], BF16), "q": ([DH, T], BF16),
                "k": ([DH, T], BF16), "stats": ([4, T], F32),
                "xn_tm": ([128, 16 * 1024], BF16), "gsc_sb": ([8, T], F32),
                "gsc_tm": ([128, 128], F32), "g8": ([128, 8], F32),
                "gsc4": ([128, 4], F32), "ctm": ([128, 8], F32),
                "top8d": ([128, 8], F32),
            }[name]
            self.d_dump[name] = dt("dump_" + name, shp, dtp, kind="ExternalOutput").ap()

    def _declare_sbuf(self):
        nc, E = self.nc, self.es.enter_context
        self.x_fm = E(nc.sbuf_tensor("x_fm", [128, NCH * T], F32)).ap()
        self.xn = E(nc.sbuf_tensor("xn", [128, NCH * T], BF16)).ap()
        self.w1p = E(nc.sbuf_tensor("w1p", [128, 2 * 1024], BF16)).ap()
        self.w3p = E(nc.sbuf_tensor("w3p", [128, 2 * 1024], BF16)).ap()
        self.w2p = E(nc.sbuf_tensor("w2p", [128, 2 * 2048], BF16)).ap()
        self.gb = E(nc.sbuf_tensor("gbt", [128, 16], F32)).ap()
        self.wqt = E(nc.sbuf_tensor("wqt", [128, NCH * DH], BF16)).ap()
        self.wkt = E(nc.sbuf_tensor("wkt", [128, NCH * DH], BF16)).ap()
        self.gwt = E(nc.sbuf_tensor("gwt", [128, NCH * NE], BF16)).ap()
        self.tri = E(nc.sbuf_tensor("tri", [128, 128], F32)).ap()
        self.identf = E(nc.sbuf_tensor("identf", [128, 128], F32)).ap()
        self.identb = E(nc.sbuf_tensor("identb", [128, 128], BF16)).ap()
        self.ones_f = E(nc.sbuf_tensor("ones_f", [128, 1], F32)).ap()
        self.ones_b = E(nc.sbuf_tensor("ones_b", [128, 1], BF16)).ap()
        self.ones_k1 = E(nc.sbuf_tensor("ones_k1", [1, 128], F32)).ap()
        self.ones_k1b = E(nc.sbuf_tensor("ones_k1b", [1, 128], BF16)).ap()
        self.negb = E(nc.sbuf_tensor("negb", [128, 1], F32)).ap()
        self.scal = E(nc.sbuf_tensor("scal", [128, 4], F32)).ap()
        self.scbuf = E(nc.sbuf_tensor("scbuf", [1, 4], F32)).ap()
        self.lmls = E(nc.sbuf_tensor("lmls", [1, 2], F32)).ap()
        self.top8 = E(nc.sbuf_tensor("top8", [128, 16], F32)).ap()
        self.thr = E(nc.sbuf_tensor("thr", [128, 2], F32)).ap()
        self.negmax = E(nc.sbuf_tensor("negmax", [128, 2], F32)).ap()
        self.rsum = E(nc.sbuf_tensor("rsum", [128, 2], F32)).ap()
        self.rinv = E(nc.sbuf_tensor("rinv", [128, 2], F32)).ap()
        self.g8 = E(nc.sbuf_tensor("g8", [128, 8], F32)).ap()
        self.cvals = E(nc.sbuf_tensor("cvals", [128, 4], F32)).ap()
        self.gsc = E(nc.sbuf_tensor("gsc", [128, 4], F32)).ap()
        self.sel = E(nc.sbuf_tensor("sel", [8, 1024], BF16)).ap()
        AR = 90 * 1024
        self.arena = E(nc.sbuf_tensor("arena", [128, AR], U8)).ap()
        a = self.arena

        def vf32(off, n):
            return a[:, off:off + 4 * n].bitcast(F32)

        def vbf(off, n):
            return a[:, off:off + 2 * n].bitcast(BF16)

        self.xn_tm = vbf(0, 16 * 1024)
        self.attnT = vbf(32768, 16 * TW)
        self.simb = vf32(49152, T)
        self.simb2 = vf32(83968, T)
        self.scorem = vf32(57344, T)
        self.attnb = vbf(65536, T)
        self.mask8 = a[:, 69632:69632 + T]
        self.q_sb = vbf(71680, T)[0:DH, :]
        self.k_sb = vbf(75776, T)[0:DH, :]
        self.btmp = vf32(79872, TW)
        self.gelo = vf32(81920, TW)
        self.c_em = vbf(0, T)[0:8, :]
        self.c_rep = vf32(8192, T)
        self.gsc_sb = vf32(16384, T)[0:8, :]
        self.gsc_tm = vf32(24576, 128)
        self.ctm = vf32(25088, 128)
        self.g8w = vf32(33792, 128)[:, 0:128]
        self.m12 = vf32(34304, 32)
        self.gsc16 = vf32(34432, 16)
        self.s1b = vbf(25600, 4 * TW)
        self.x3b = vbf(29696, 4 * TW)
        self.h_half = vbf(34816, NK * T)
        self.x2buf = vbf(0, T)
        self.rstd_sb = vf32(4096, T)
        self.xcast = [vbf(4096, T), vbf(8192, T)]
        self.st_mean = vf32(12288, T)[0:1, :]
        self.st_rstd = vf32(20480, T)[0:1, :]
        self.st_tmp = vf32(12288, T)[64:65, :]
        self.st_mean_b = vbf(83968, T)[0:1, :]
        self.st_rstd_b = vbf(88064, T)[0:1, :]

    def _declare_psum(self):
        nc, E = self.nc, self.es.enter_context
        self.ps = [E(nc.psum_tensor(f"ps{i}", [128, TW], F32)).ap() for i in range(8)]
        self._psTb = [b.bitcast(BF16) for b in self.ps]

    def psT(self, sl, banks=(6, 7)):
        bank = self._psTb[banks[sl % 2]]
        j = (sl // 2) % 4
        return bank[:, j * 128:(j + 1) * 128]

    def mm(self, out, lhsT, rhs, start, stop, waits=(), inc=None):
        return self.p.emit(
            "tensor",
            lambda t, out=out, lhsT=lhsT, rhs=rhs, start=start, stop=stop:
                t.matmul(out, lhsT, rhs, start=start, stop=stop),
            waits=waits, inc=inc)

    def tp(self, out, in_, ident, waits=(), inc=None):
        return self.p.emit(
            "tensor",
            lambda t, out=out, in_=in_, ident=ident: t.transpose(out, in_, ident),
            waits=waits, inc=inc)

    def prologue(self):
        p = self.p
        self.tok_gb = p.emit("sync", lambda s: s.dma_start(out=self.gb, in_=self.d_gb),
                             inc="Xgb")
        tok_lm = p.emit("sync", lambda s: s.dma_start(out=self.lmls[:, 0:1], in_=self.d_lm),
                        inc="Xlm")
        tok_ls = p.emit("sync", lambda s: s.dma_start(out=self.lmls[:, 1:2], in_=self.d_ls),
                        inc="Xls")
        self.tok_x = []
        for c in range(NCH):
            self.tok_x.append(p.emit(
                "sync" if c % 2 == 0 else "scalar",
                lambda s, c=c: s.dma_start(
                    out=self.x_fm[:, c * T:(c + 1) * T], in_=self.d_x[:, c]),
                inc=f"Xx{c}"))
        self.xcast_free = None
        self.tok_wq = p.emit(
            "gpsimd", lambda g: g.dma_start(out=self.wqt,
                                            in_=self.d_wq.rearrange("p c h -> p (c h)")),
            inc="Wq")
        self.tok_wk = p.emit(
            "gpsimd", lambda g: g.dma_start(out=self.wkt,
                                            in_=self.d_wk.rearrange("p c h -> p (c h)")),
            inc="Wk")
        self.tok_gw = p.emit(
            "gpsimd", lambda g: g.dma_start(out=self.gwt,
                                            in_=self.d_gw.rearrange("p c e -> p (c e)")),
            inc="Wg")
        p.emit("gpsimd", lambda g: g.memset(self.ones_f, 1.0), inc="GP")
        p.emit("gpsimd", lambda g: g.memset(self.ones_b, 1.0), inc="GP")
        p.emit("gpsimd", lambda g: g.memset(self.ones_k1, 1.0), inc="GP")
        p.emit("gpsimd", lambda g: g.memset(self.ones_k1b, 1.0), inc="GP")
        p.emit("gpsimd", lambda g: g.memset(self.negb, NEGMASK), inc="GP")
        tif = p.emit("gpsimd", lambda g: g.memset(self.identf, 1.0), inc="GP")
        p.emit("gpsimd", lambda g: g.affine_select(
            out=self.identf, in_=self.identf, pattern=[[-1, 128]],
            channel_multiplier=1, base=0, compare_op=ALU.is_equal, fill=0.0),
            waits=[tif], inc="GP")
        tib = p.emit("gpsimd", lambda g: g.memset(self.identb, 1.0), inc="GP")
        p.emit("gpsimd", lambda g: g.affine_select(
            out=self.identb, in_=self.identb, pattern=[[-1, 128]],
            channel_multiplier=1, base=0, compare_op=ALU.is_equal, fill=0.0),
            waits=[tib], inc="GP")
        p.emit("gpsimd", lambda g: g.memset(self.cvals[:, 0:1], EPS), inc="GP")
        p.emit("gpsimd", lambda g: g.memset(self.cvals[:, 1:2], -0.5), inc="GP")
        p.emit("gpsimd", lambda g: g.memset(self.cvals[:, 2:3], RSQ_DH), inc="GP")
        p.emit("gpsimd", lambda g: g.memset(self.cvals[:, 3:4], -1.0), inc="GP")
        ttr = p.emit("gpsimd", lambda g: g.memset(self.tri, 0.0), inc="GP")
        self.tok_consts = p.emit("gpsimd", lambda g: g.affine_select(
            out=self.tri, in_=self.tri, pattern=[[-1, 128]],
            channel_multiplier=1, base=-1, compare_op=ALU.is_ge, fill=NEGBIG),
            waits=[ttr], inc="GP")
        tz = p.emit("gpsimd", lambda g: g.memset(self.scbuf, 0.0), inc="GP")
        t1 = p.emit("scalar",
                    lambda s: s.activation(self.scbuf[:, 0:1], self.lmls[:, 0:1], AF.Sigmoid),
                    waits=[tok_lm, tz], inc="SC")
        t2 = p.emit("vector",
                    lambda v: v.tensor_scalar(self.scbuf[:, 1:2], self.scbuf[:, 0:1],
                                              -1.0, 1.0, ALU.mult, ALU.add),
                    waits=[t1], inc="VE")
        t3a = p.emit("scalar",
                     lambda s: s.activation(self.scbuf[:, 2:3], self.lmls[:, 1:2], AF.Exp),
                     waits=[tok_ls, tz], inc="SC")
        t3b = p.emit("vector",
                     lambda v: v.tensor_scalar(self.scbuf[:, 2:3], self.scbuf[:, 2:3],
                                               1.0, None, ALU.add),
                     waits=[t3a], inc="VE")
        t3 = p.emit("scalar",
                    lambda s: s.activation(self.scbuf[:, 2:3], self.scbuf[:, 2:3], AF.Ln),
                    waits=[t3b], inc="SC")
        t4 = p.emit("vector",
                    lambda v: v.tensor_scalar(self.scbuf[:, 2:3], self.scbuf[:, 2:3],
                                              0.01, None, ALU.add),
                    waits=[t3], inc="VE")
        t5 = self.mm(self.ps[6][:, 0:4], self.ones_k1, self.scbuf, True, True,
                     waits=[t2, t4, self.tok_consts], inc="PE")
        self.tok_scal = p.emit("vector", lambda v: v.tensor_copy(self.scal, self.ps[6][:, 0:4]),
                               waits=[t5], inc="VE")
        self.bank_free[6] = self.tok_scal

    def layernorm(self, dep):
        p, ps = self.p, self.ps
        sum_toks = [None] * NJ
        cast_free = [self.xcast_free, self.xcast_free]
        for c in range(NCH):
            xc = self.xcast[c % 2]
            tc_ = p.emit("vector",
                         lambda v, c=c, xc=xc: v.tensor_copy(
                             xc, self.x_fm[:, c * T:(c + 1) * T]),
                         waits=[dep(c, 0), dep(c, 1), dep(c, 2), dep(c, 3),
                                cast_free[c % 2]], inc="VE")
            for j in range(NJ):
                tok = self.mm(ps[j][0:1, :], self.ones_b,
                              xc[:, j * TW:(j + 1) * TW],
                              start=(c == 0), stop=(c == NCH - 1),
                              waits=[tc_, self.bank_free[j]], inc="PE")
                sum_toks[j] = tok
            cast_free[c % 2] = sum_toks[-1]
        self.xcast_free = sum_toks[-1]
        sq_tile_toks = [None] * NJ
        prev_sq_mm = None
        for c in range(NCH):
            tsq = p.emit("scalar",
                         lambda s, c=c: s.activation(self.x2buf,
                                                     self.x_fm[:, c * T:(c + 1) * T],
                                                     AF.Square),
                         waits=[dep(c, 0), dep(c, 1), dep(c, 2), dep(c, 3),
                                prev_sq_mm if c > 0 else self.sq_free], inc="SC")
            for j in range(NJ):
                tok = self.mm(ps[4 + j][0:1, :], self.ones_b,
                              self.x2buf[:, j * TW:(j + 1) * TW],
                              start=(c == 0), stop=(c == NCH - 1),
                              waits=[tsq, self.bank_free[4 + j]], inc="PE")
                sq_tile_toks[j] = tok
                prev_sq_mm = tok
        ep = []
        for j in range(NJ):
            sl = slice(j * TW, (j + 1) * TW)
            t1 = p.emit("vector",
                        lambda v, j=j, sl=sl: v.tensor_scalar(
                            self.st_mean[:, sl], ps[j][0:1, :], -1.0 / D, None, ALU.mult),
                        waits=[sum_toks[j]], inc="VE")
            t2 = p.emit("vector",
                        lambda v, sl=sl: v.tensor_tensor(
                            self.st_tmp[:, sl], self.st_mean[:, sl], self.st_mean[:, sl],
                            ALU.mult),
                        waits=[t1], inc="VE")
            t3 = p.emit("vector",
                        lambda v, j=j, sl=sl: v.scalar_tensor_tensor(
                            self.st_tmp[:, sl], ps[4 + j][0:1, :], 1.0 / D,
                            self.st_tmp[:, sl], ALU.mult, ALU.subtract),
                        waits=[t2, sq_tile_toks[j]], inc="VE")
            t4 = p.emit("scalar",
                        lambda s, sl=sl: s.activation(self.st_rstd[:, sl],
                                                      self.st_tmp[:, sl], AF.Ln,
                                                      bias=self.cvals[0:1, 0:1]),
                        waits=[t3, self.tok_consts], inc="SC")
            t5 = p.emit("scalar",
                        lambda s, sl=sl: s.activation(self.st_rstd[:, sl],
                                                      self.st_rstd[:, sl], AF.Exp,
                                                      scale=self.cvals[0:1, 1:2]),
                        waits=[t4], inc="SC")
            t6 = p.emit("vector",
                        lambda v, sl=sl: v.tensor_copy(self.st_mean_b[:, sl],
                                                       self.st_mean[:, sl]),
                        waits=[t3], inc="VE")
            t7 = p.emit("vector",
                        lambda v, sl=sl: v.tensor_copy(self.st_rstd_b[:, sl],
                                                       self.st_rstd[:, sl]),
                        waits=[t5], inc="VE")
            ep.append((t6, t3, t7))
        rep_r = []
        for j in range(NJ):
            sl = slice(j * TW, (j + 1) * TW)
            tr = self.mm(ps[4 + j][:, :], self.ones_k1b, self.st_rstd_b[:, sl], True, True,
                         waits=[ep[j][2], ep[j][1]], inc="PE")
            tre = p.emit("vector",
                         lambda v, j=j, sl=sl: v.tensor_copy(self.rstd_sb[:, sl],
                                                             ps[4 + j][:, :]),
                         waits=[tr, self.xcast_free], inc="VE")
            rep_r.append(tre)
            self.bank_free[4 + j] = tre
        last = None
        lastm = [None] * NJ
        xn_cj = [[None] * NJ for _ in range(NCH)]
        for c in range(NCH):
            for j in range(NJ):
                sl_x = slice(c * T + j * TW, c * T + (j + 1) * TW)
                sl_t = slice(j * TW, (j + 1) * TW)
                tcp = self.mm(ps[j][:, :], self.identf,
                              self.x_fm[:, sl_x], True, False,
                              waits=[dep(c, j), lastm[j], sum_toks[j]])
                tmm = self.mm(ps[j][:, :], self.ones_k1b, self.st_mean_b[:, sl_t],
                              False, True, waits=[ep[j][0]], inc="PE")
                t2 = p.emit("vector",
                            lambda v, sl_x=sl_x, sl_t=sl_t, j=j: v.tensor_tensor(
                                self.xn[:, sl_x], ps[j][:, :], self.rstd_sb[:, sl_t],
                                ALU.mult),
                            waits=[tmm, rep_r[j]], inc="VE")
                lastm[j] = t2
                xn_cj[c][j] = t2
                last = t2
        for j in range(NJ):
            self.bank_free[j] = lastm[j]
        self.sq_free = prev_sq_mm
        return xn_cj, last

    def gate(self, xn_cj):
        p, ps = self.p, self.ps
        tsel0 = p.emit("gpsimd", lambda g: g.memset(self.sel, 1.0),
                       waits=[getattr(self, "sel_free", None)], inc="GP")
        tok_sel = p.emit("gpsimd", lambda g: g.affine_select(
            out=self.sel, in_=self.sel, pattern=[[1, 8], [0, 128]],
            channel_multiplier=-1, base=0, compare_op=ALU.is_equal, fill=0.0),
            waits=[tsel0], inc="GP")
        self.tok_sel = tok_sel
        sc_toks = []
        for j in range(NJ):
            bank = j % 2
            tok = None
            for c in range(NCH):
                tok = self.mm(ps[bank][0:8, :],
                              self.gwt[:, c * NE:(c + 1) * NE],
                              self.xn[:, c * T + j * TW: c * T + (j + 1) * TW],
                              start=(c == 0), stop=(c == NCH - 1),
                              waits=[xn_cj[c][j], self.tok_gw, self.bank_free[bank]],
                              inc="PE" if c == NCH - 1 else None)
            ev = p.emit("vector",
                        lambda v, j=j, bank=bank: v.tensor_copy(
                            self.gsc_sb[:, j * TW:(j + 1) * TW], ps[bank][0:8, :]),
                        waits=[tok], inc="VE")
            self.bank_free[bank] = ev
            sc_toks.append(ev)
        last = None
        tm_toks = []
        for b in range(16):
            j = (b * 128) // TW
            bank = 2 + b % 2
            tt = self.tp(ps[bank][:, 0:8],
                         self.gsc_sb[:, b * 128:(b + 1) * 128], self.identf[0:8, 0:8],
                         waits=[sc_toks[j], self.bank_free[bank]], inc="PE")
            ev = p.emit("vector",
                        lambda v, b=b, bank=bank: v.tensor_copy(
                            self.gsc_tm[:, b * 8:(b + 1) * 8], ps[bank][:, 0:8]),
                        waits=[tt], inc="VE")
            self.bank_free[bank] = ev
            tm_toks.append(ev)
        m_toks = []
        prev_cp = None
        for b in range(16):
            t1 = p.emit("vector",
                        lambda v, b=b: v.max(self.g8, self.gsc_tm[:, b * 8:(b + 1) * 8]),
                        waits=[tm_toks[b], prev_cp], inc="VE")
            p.emit("vector", lambda v: v.engine_nop())
            cp = p.emit("vector",
                        lambda v, b=b: v.tensor_copy(self.m12[:, 2 * b:2 * b + 2],
                                                     self.g8[:, 0:2]),
                        waits=[t1], inc="VE")
            prev_cp = cp
            m_toks.append(cp)
        m1a = self.m12[:, 0:32:2].unsqueeze(2).broadcast_to((128, 16, 8))
        m2a = self.m12[:, 1:32:2].unsqueeze(2).broadcast_to((128, 16, 8))
        t_sub = p.emit("vector",
                       lambda v: v.tensor_tensor(self.ctm, self.gsc_tm, m1a,
                                                 ALU.subtract),
                       waits=m_toks, inc="VE")
        t_exp = p.emit("scalar",
                       lambda s: s.activation(self.ctm, self.ctm, AF.Exp),
                       waits=[t_sub], inc="SC")
        t_msk = p.emit("vector",
                       lambda v: v.tensor_tensor(self.g8w, self.gsc_tm, m2a, ALU.is_ge),
                       waits=m_toks + [t_sub], inc="VE")
        t_d1 = p.emit("vector",
                      lambda v: v.tensor_tensor(self.gsc16, self.m12[:, 1:32:2],
                                                self.m12[:, 0:32:2], ALU.subtract),
                      waits=m_toks, inc="VE")
        t_d2 = p.emit("scalar", lambda s: s.activation(self.gsc16, self.gsc16, AF.Exp),
                      waits=[t_d1], inc="SC")
        t_d3 = p.emit("vector",
                      lambda v: v.tensor_scalar(self.gsc16, self.gsc16, 1.0, None,
                                                ALU.add),
                      waits=[t_d2], inc="VE")
        t_d4 = p.emit("scalar", lambda s: s.activation(self.gsc16, self.gsc16, AF.Ln),
                      waits=[t_d3], inc="SC")
        t_d5 = p.emit("scalar",
                      lambda s: s.activation(self.gsc16, self.gsc16, AF.Exp,
                                             scale=self.cvals[:, 3:4]),
                      waits=[t_d4, self.tok_consts], inc="SC")
        t_d6 = p.emit("vector",
                      lambda v: v.tensor_scalar(self.gsc16, self.gsc16, 0.5, None,
                                                ALU.mult),
                      waits=[t_d5], inc="VE")
        cca = self.gsc16.unsqueeze(2).broadcast_to((128, 16, 8))
        t_c1 = p.emit("vector",
                      lambda v: v.tensor_tensor(self.ctm, self.ctm, self.g8w, ALU.mult),
                      waits=[t_exp, t_msk], inc="VE")
        t_c2 = p.emit("vector",
                      lambda v, cca=cca: v.tensor_tensor(self.ctm, self.ctm, cca,
                                                         ALU.mult),
                      waits=[t_c1, t_d6], inc="VE")
        for b in range(16):
            bank2 = 4 + b % 2
            tt2 = self.tp(ps[bank2][0:8, 0:128], self.ctm[:, b * 8:(b + 1) * 8],
                          self.identf,
                          waits=[t_c2, self.bank_free[bank2]], inc="PE")
            ev2 = p.emit("vector",
                         lambda v, b=b, bank2=bank2: v.tensor_copy(
                             self.c_em[:, b * 128:(b + 1) * 128], ps[bank2][0:8, 0:128]),
                         waits=[tt2], inc="VE")
            self.bank_free[bank2] = ev2
            last = ev2
        return last

    def ffn(self, xn_cj, c_tok, eh_base):
        p, ps = self.p, self.ps
        last_evac = None
        tile_final = [[None] * NJ for _ in range(NCH)]
        for e in range(NE):
            crep_toks = []
            for j in range(NJ):
                bank = j % 2
                tm = self.mm(ps[bank][:, :], self.sel[:, e * 128:(e + 1) * 128],
                             self.c_em[:, j * TW:(j + 1) * TW], True, True,
                             waits=[c_tok, self.tok_sel, self.bank_free[bank]], inc="PE")
                ev = p.emit("vector",
                            lambda v, j=j, bank=bank: v.tensor_copy(
                                self.c_rep[:, j * TW:(j + 1) * TW], ps[bank][:, :]),
                            waits=[tm, self.crep_free], inc="VE")
                self.bank_free[bank] = ev
                crep_toks.append(ev)
            crep_ready = crep_toks[-1]
            for half in range(2):
                eh = eh_base + e * 2 + half
                h_tile_last = [None] * NJ
                for k in range(NK):
                    slot = (eh * NK + k) % 2
                    w1sl = self.w1p[:, slot * 1024:(slot + 1) * 1024]
                    w3sl = self.w3p[:, slot * 1024:(slot + 1) * 1024]
                    tok_w1 = p.emit(
                        "gpsimd",
                        lambda g, e=e, half=half, k=k, w1sl=w1sl: g.dma_start(
                            out=w1sl, in_=self.d_w1[e, half, k].rearrange("p c j -> p (c j)")),
                        waits=[self.w1_free[slot]], inc="W1a" if slot == 0 else "W1b")
                    tok_w3 = p.emit(
                        "gpsimd",
                        lambda g, e=e, half=half, k=k, w3sl=w3sl: g.dma_start(
                            out=w3sl, in_=self.d_w3[e, half, k].rearrange("p c j -> p (c j)")),
                        waits=[self.w3_free[slot]], inc="W3a" if slot == 0 else "W3b")
                    x1_stop = [None] * NJ
                    x3_stop = [None] * NJ
                    lastmm1 = lastmm3 = None
                    g3 = self.bank_free[7]
                    for j in range(NJ):
                        for c in range(NCH):
                            lastmm1 = self.mm(
                                ps[j][:, :], w1sl[:, c * 128:(c + 1) * 128],
                                self.xn[:, c * T + j * TW: c * T + (j + 1) * TW],
                                start=(c == 0), stop=(c == NCH - 1),
                                waits=[xn_cj[c][j], tok_w1, self.bank_free[j]],
                                inc="PE" if c == NCH - 1 else None)
                        x1_stop[j] = lastmm1
                    for j in range(NJ):
                        for c in range(NCH):
                            lastmm3 = self.mm(
                                ps[4 + j][:, :], w3sl[:, c * 128:(c + 1) * 128],
                                self.xn[:, c * T + j * TW: c * T + (j + 1) * TW],
                                start=(c == 0), stop=(c == NCH - 1),
                                waits=[xn_cj[c][j], tok_w3, g3],
                                inc="PE" if c == NCH - 1 else None)
                        x3_stop[j] = lastmm3
                    for j in range(NJ):
                        s1sl = self.s1b[:, j * TW:(j + 1) * TW]
                        x3sl = self.x3b[:, j * TW:(j + 1) * TW]
                        ts1 = p.emit("scalar",
                                     lambda s, s1sl=s1sl, j=j: s.activation(
                                         s1sl, ps[j][:, :], AF.Silu),
                                     waits=[x1_stop[j], self.s1_free[j]], inc="SC")
                        tx3 = p.emit("vector",
                                     lambda v, x3sl=x3sl, j=j: v.tensor_tensor(
                                         x3sl, ps[4 + j][:, :],
                                         self.c_rep[:, j * TW:(j + 1) * TW], ALU.mult),
                                     waits=[x3_stop[j], crep_ready, self.x3_free[j]],
                                     inc="VE")
                        th = p.emit("vector",
                                    lambda v, k=k, j=j, s1sl=s1sl, x3sl=x3sl: v.tensor_tensor(
                                        self.h_half[:, k * T + j * TW: k * T + (j + 1) * TW],
                                        s1sl, x3sl, ALU.mult),
                                    waits=[ts1, tx3, self.h_free], inc="VE")
                        self.bank_free[j] = ts1
                        self.bank_free[4 + j] = tx3
                        self.s1_free[j] = th
                        self.x3_free[j] = th
                        h_tile_last[j] = th
                        self.tx3_last = tx3
                    self.w1_free[slot] = lastmm1
                    self.w3_free[slot] = lastmm3
                g = 0
                lastw2 = None
                for dp in range(4):
                    slot = (eh * 4 + dp) % 2
                    w2sl = self.w2p[:, slot * 2048:(slot + 1) * 2048]
                    tok_w2 = p.emit(
                        "gpsimd",
                        lambda g_, e=e, half=half, dp=dp, w2sl=w2sl: g_.dma_start(
                            out=w2sl, in_=self.d_w2[e, half, dp].rearrange("p k q -> p (k q)")),
                        waits=[self.w2_free[slot]], inc="W2a" if slot == 0 else "W2b")
                    for j in range(NJ):
                        for di in range(2):
                            d = dp * 2 + di
                            bank = 4 + g % 4
                            g += 1
                            tok = None
                            for k in range(NK):
                                tok = self.mm(
                                    ps[bank][:, :],
                                    w2sl[:, k * 256 + di * 128: k * 256 + (di + 1) * 128],
                                    self.h_half[:, k * T + j * TW: k * T + (j + 1) * TW],
                                    start=(k == 0), stop=(k == NK - 1),
                                    waits=[tok_w2, self.bank_free[bank], h_tile_last[j]],
                                    inc="PE" if k == NK - 1 else None)
                            lastw2 = tok
                            ev = p.emit(
                                "vector",
                                lambda v, d=d, j=j, bank=bank: v.tensor_tensor(
                                    self.x_fm[:, d * T + j * TW: d * T + (j + 1) * TW],
                                    self.x_fm[:, d * T + j * TW: d * T + (j + 1) * TW],
                                    ps[bank][:, :], ALU.add),
                                waits=[tok], inc="VE")
                            self.bank_free[bank] = ev
                            last_evac = ev
                            tile_final[d][j] = ev
                    self.w2_free[slot] = lastw2
                self.h_free = lastw2
            self.crep_free = self.tx3_last
        return tile_final, last_evac

    def attention(self, xn_cj, xn_tok):
        p, ps = self.p, self.ps
        tile_final = [[None] * NJ for _ in range(NCH)]
        qk_toks = []
        for which in range(2):
            wt = self.wqt if which == 0 else self.wkt
            dst_t = self.q_sb if which == 0 else self.k_sb
            scale = self.cvals[0:DH, 2:3] if which == 0 else 1.0
            for j in range(NJ):
                tok = None
                for c in range(NCH):
                    tok = self.mm(ps[0][0:DH, :], wt[:, c * DH:(c + 1) * DH],
                                  self.xn[:, c * T + j * TW: c * T + (j + 1) * TW],
                                  start=(c == 0), stop=(c == NCH - 1),
                                  waits=[xn_cj[c][j], self.tok_wq, self.tok_wk,
                                         self.bank_free[0]],
                                  inc="PE" if c == NCH - 1 else None)
                ev = p.emit("scalar",
                            lambda s, dst_t=dst_t, j=j, scale=scale: s.activation(
                                dst_t[:, j * TW:(j + 1) * TW], ps[0][0:DH, :],
                                AF.Copy, scale=scale),
                            waits=[tok], inc="SC")
                self.bank_free[0] = ev
                qk_toks.append(ev)
        qk_ready = qk_toks[-1]
        xn_tm_last = None
        i = 0
        for sc in range(16):
            for c in range(NCH):
                sl = i % 8
                i += 1
                tt = self.tp(self.psT(sl, (4, 1)),
                             self.xn[:, c * T + sc * 128:c * T + (sc + 1) * 128],
                             self.identb,
                             waits=[xn_cj[c][sc // 4], self.slot_free[sl % 2],
                                    self.bank_free[4], self.bank_free[1]],
                             inc="PE")
                ev = p.emit("scalar",
                            lambda s, sc=sc, c=c, sl=sl: s.activation(
                                self.xn_tm[:, sc * 1024 + c * 128: sc * 1024 + (c + 1) * 128],
                                self.psT(sl, (4, 1)), AF.Copy),
                            waits=[tt], inc="SC")
                self.slot_free[sl % 2] = ev
                xn_tm_last = ev
        self.bank_free[0] = xn_tm_last
        self.bank_free[1] = xn_tm_last
        self.bank_free[4] = xn_tm_last
        prev_exp = None
        prev_rinv = None
        prev_tp_last = None
        prev_tth = None
        prev_sel = None
        sim_free2 = [None, None]
        last_x = xn_tm_last
        st = {"prev_exp": prev_exp, "prev_rinv": [None, None],
              "prev_tp_last": prev_tp_last, "prev_tth": [None, None],
              "prev_mask": None, "tab": {}, "attnT_sc": {}, "attnT_ve": {},
              "last_x": last_x, "msg_last": None}
        sim_free2 = [None, None]

        def emit_chain(b):
            par = b % 2
            simb = self.simb if par == 0 else self.simb2
            jmax = b // 4
            W = T if b == 0 else (jmax + 1) * TW
            sim_ev = []
            for j in range(jmax + 1):
                bank = j % 2
                tok = None
                for c in range(NCH):
                    tok = self.mm(ps[bank][:, :],
                                  self.xn[:, c * T + b * 128: c * T + (b + 1) * 128],
                                  self.xn[:, c * T + j * TW: c * T + (j + 1) * TW],
                                  start=(c == 0), stop=(c == NCH - 1),
                                  waits=[xn_cj[c][b // 4], xn_cj[c][j],
                                         self.bank_free[bank]],
                                  inc="PE" if c == NCH - 1 else None)
                ev = p.emit("scalar",
                            lambda s, j=j, bank=bank, simb=simb: s.activation(
                                simb[:, j * TW:(j + 1) * TW], ps[bank][:, :],
                                AF.Copy),
                            waits=[tok, sim_free2[par]], inc="SC")
                self.bank_free[bank] = ev
                sim_ev.append(ev)
            td = p.emit("vector",
                        lambda v, b=b, simb=simb: v.tensor_tensor(
                            simb[:, b * 128:(b + 1) * 128],
                            simb[:, b * 128:(b + 1) * 128], self.tri, ALU.add),
                        waits=[sim_ev[-1], self.tok_consts, sim_free2[par]],
                        inc="VE")
            tu = None
            wmask = T if b == 0 else (jmax + 1) * TW
            if (b + 1) * 128 < wmask:
                tu = p.emit("vector",
                            lambda v, b=b, wmask=wmask, simb=simb: v.memset(
                                simb[:, (b + 1) * 128: wmask], NEGBIG),
                            waits=[td], inc="VE")
            wmax = (b + 1) * 128
            t8s = self.top8[:, par * 8:par * 8 + 8]
            thr = self.thr[:, par:par + 1]
            nmx = self.negmax[:, par:par + 1]
            rsm = self.rsum[:, par:par + 1]
            rnv = self.rinv[:, par:par + 1]
            t8 = p.emit("vector",
                        lambda v, wmax=wmax, simb=simb, t8s=t8s: v.max(
                            t8s, simb[:, 0:wmax]),
                        waits=[td, tu, st["prev_tth"][par]] + sim_ev, inc="VE")
            p.emit("vector", lambda v: v.engine_nop())
            tth = p.emit("vector",
                         lambda v, t8s=t8s, thr=thr: v.tensor_scalar(
                             thr, t8s[:, 7:8], THRFLOOR, None, ALU.max),
                         waits=[t8], inc="VE")
            st["prev_tth"][par] = tth
            mask_b = self.mask8
            if b == 0:
                tmk = p.emit("vector",
                             lambda v, W=W, simb=simb, thr=thr, mask_b=mask_b:
                             v.tensor_single_scalar(
                                 mask_b[:, 0:W], simb[:, 0:W],
                                 thr[:, 0:1], ALU.is_ge),
                             waits=[tth, st["prev_mask"]], inc="VE")
            else:
                tmk = p.emit("vector",
                             lambda v, W=W, simb=simb, thr=thr, mask_b=mask_b:
                             v.tensor_single_scalar(
                                 mask_b[:, 0:W], simb[:, 0:W],
                                 thr[:, 0:1], ALU.is_lt),
                             waits=[tth, st["prev_mask"]], inc="VE")
            sim_free2[par] = tmk
            jmaxb = b // 4
            sel_toks = []
            for j in range(jmaxb + 1):
                bank = 4 + j % 2
                tsc = self.mm(ps[bank][:, :],
                              self.q_sb[:, b * 128:(b + 1) * 128],
                              self.k_sb[:, j * TW:(j + 1) * TW],
                              True, True,
                              waits=[qk_ready, self.bank_free[bank]], inc="PE")
                if b == 0:
                    tse = p.emit("vector",
                                 lambda v, j=j, bank=bank, mask_b=mask_b: v.select(
                                     self.scorem[:, j * TW:(j + 1) * TW],
                                     mask_b[:, j * TW:(j + 1) * TW],
                                     ps[bank][:, :],
                                     self.negb[:, 0:1].broadcast_to((128, TW)),
                                     add_drain=True),
                                 waits=[tmk, tsc, st["prev_exp"]], inc="VE")
                else:
                    tse = p.emit("vector",
                                 lambda v, j=j, bank=bank, mask_b=mask_b:
                                 v.scalar_tensor_tensor(
                                     self.scorem[:, j * TW:(j + 1) * TW],
                                     mask_b[:, j * TW:(j + 1) * TW], NEGMASK,
                                     ps[bank][:, :], ALU.mult, ALU.add),
                                 waits=[tmk, tsc, st["prev_exp"]], inc="VE")
                self.bank_free[bank] = tse
                sel_toks.append(tse)
            if b == 0:
                for j in range(1, 4):
                    tse = p.emit("vector",
                                 lambda v, j=j: v.memset(
                                     self.scorem[:, j * TW:(j + 1) * TW], NEGMASK),
                                 waits=[tmk, st["prev_exp"]], inc="VE")
                    sel_toks.append(tse)
            st["prev_mask"] = sel_toks[-1]
            trm = p.emit("vector",
                         lambda v, W=W, nmx=nmx: v.reduce_max(
                             nmx, self.scorem[:, 0:W], axis=AX.X, negate=True),
                         waits=sel_toks + [st["prev_exp"]], inc="VE")
            texp = p.emit("scalar",
                          lambda s, W=W, nmx=nmx, rsm=rsm: s.activation(
                              self.attnb[:, 0:W], self.scorem[:, 0:W], AF.Exp,
                              bias=nmx[:, 0:1], accum_out=rsm[:, 0:1]),
                          waits=[trm, st["prev_rinv"][par], st["prev_tp_last"]],
                          inc="SC")
            st["prev_exp"] = texp
            tra = p.emit("scalar",
                         lambda s, rnv=rnv, rsm=rsm: s.activation(rnv, rsm, AF.Ln),
                         waits=[texp], inc="SC")
            tri_ = p.emit("scalar",
                          lambda s, rnv=rnv: s.activation(rnv, rnv, AF.Exp,
                                                          scale=self.cvals[:, 3:4]),
                          waits=[tra], inc="SC")
            st["prev_rinv"][par] = tri_
            tab = p.emit("vector",
                         lambda v, W=W, rnv=rnv: v.tensor_scalar(
                             self.attnb[:, 0:W], self.attnb[:, 0:W],
                             rnv[:, 0:1], None, ALU.mult),
                         waits=[tri_, st["prev_tp_last"]], inc="VE")
            st["tab"][b] = tab

        def emit_tp(b):
            S = b // 4
            bb = b % 4
            jmax = b // 4
            W = T if b == 0 else (jmax + 1) * TW
            npref = W // 128
            sc_top = 16 if S == 0 else 4 * (S + 1)
            tab = st["tab"][b]
            for sc in range(sc_top):
                sl = sc % 8
                if sc < npref:
                    tt = self.tp(self.psT(sl),
                                 self.attnb[:, sc * 128:(sc + 1) * 128],
                                 self.identb,
                                 waits=[tab, self.slot_free[sl % 2]], inc="PE")
                    st["prev_tp_last"] = tt
                    ev = p.emit("scalar",
                                lambda s, sc=sc, bb=bb, sl=sl: s.activation(
                                    self.attnT[:, sc * TW + bb * 128:
                                               sc * TW + (bb + 1) * 128],
                                    self.psT(sl), AF.Copy),
                                waits=[tt, self.attnT_free], inc="SC")
                    self.slot_free[sl % 2] = ev
                    st["attnT_sc"][S] = ev
                else:
                    ev = p.emit("vector",
                                lambda v, sc=sc, bb=bb: v.memset(
                                    self.attnT[:, sc * TW + bb * 128:
                                               sc * TW + (bb + 1) * 128], 0.0),
                                waits=[self.attnT_free], inc="VE")
                    st["attnT_ve"][S] = ev

        def emit_msg(S):
            sc_top = 16 if S == 0 else 4 * (S + 1)
            for d in range(NCH):
                bank = 2 + d % 2
                tok = None
                for sc in range(sc_top):
                    tok = self.mm(ps[bank][:, :],
                                  self.xn_tm[:, sc * 1024 + d * 128:
                                             sc * 1024 + (d + 1) * 128],
                                  self.attnT[:, sc * TW:(sc + 1) * TW],
                                  start=(sc == 0), stop=(sc == sc_top - 1),
                                  waits=[st["attnT_sc"].get(S), st["attnT_ve"].get(S),
                                         xn_tm_last, self.bank_free[bank]],
                                  inc="PE" if sc == sc_top - 1 else None)
                t1 = p.emit("vector",
                            lambda v, d=d, S=S: v.tensor_scalar(
                                self.btmp, self.xn[:, d * T + S * TW: d * T + (S + 1) * TW],
                                self.scal[:, 0:1], None, ALU.mult),
                            waits=[self.tok_scal, self.btmp_free], inc="VE")
                t2 = p.emit("vector",
                            lambda v, bank=bank: v.scalar_tensor_tensor(
                                self.btmp, ps[bank][:, :], self.scal[:, 1:2],
                                self.btmp, ALU.mult, ALU.add),
                            waits=[tok, t1], inc="VE")
                self.bank_free[bank] = t2
                t3 = p.emit("scalar",
                            lambda s, d=d: s.activation(
                                self.gelo, self.btmp, AF.Gelu,
                                bias=self.gb[:, 8 + d:9 + d], scale=self.gb[:, d:d + 1]),
                            waits=[t2, self.gelo_free, self.tok_gb], inc="SC")
                self.btmp_free = t3
                t4 = p.emit("vector",
                            lambda v, d=d, S=S: v.scalar_tensor_tensor(
                                self.x_fm[:, d * T + S * TW: d * T + (S + 1) * TW],
                                self.gelo, self.scal[:, 2:3],
                                self.x_fm[:, d * T + S * TW: d * T + (S + 1) * TW],
                                ALU.mult, ALU.add),
                            waits=[t3], inc="VE")
                self.gelo_free = t4
                tile_final[d][S] = t4
                st["msg_last"] = t4
            self.attnT_free = st["msg_last"]
            st["last_x"] = st["msg_last"]

        for b in range(4 * ATTN_SB):
            emit_chain(b)
            if b % 4 == 0 and b > 0 and ATTN_MSG:
                emit_msg(b // 4 - 1)
            emit_tp(b)
        if ATTN_MSG:
            emit_msg(ATTN_SB - 1)
        last_x = st["last_x"]
        self.bank_free[6] = last_x
        self.bank_free[7] = last_x
        return tile_final, last_x

    def dump(self, name, src, wait):
        if name not in self.d_dump:
            return
        self.p.emit("sync",
                    lambda s, src=src, name=name: s.dma_start(
                        out=self.d_dump[name], in_=src),
                    waits=[wait], inc="ST")

    def epilogue(self, chunk_toks):
        p = self.p
        for c in range(NCH):
            p.emit("sync",
                   lambda s, c=c: s.dma_start(
                       out=self.d_y[:, c], in_=self.x_fm[:, c * T:(c + 1) * T]),
                   waits=[chunk_toks[c]], inc="ST")
        finals = [(s, self.p.counts[s]) for s in self.p.dma_sems
                  if self.p.counts[s] > 0]
        p.emit("sync", lambda s: s.nop(), waits=finals)


def build_program():
    nc = bass.Bass("TRN2", target_bir_lowering=False)
    with contextlib.ExitStack() as es:
        b = Builder(nc, es)
        b.prologue()
        tok = b.tok_x[-1]
        tile_tok = None
        ctok = None
        xn_cj = None
        eh_base = 0
        def coarsen(xn_cj, tok):
            if COARSE_DEPS:
                return [[tok] * NJ for _ in range(NCH)]
            return xn_cj
        if "ln1" in PHASES:
            xn_cj, tok = b.layernorm(lambda c, j: b.tok_x[c])
            xn_cj = coarsen(xn_cj, tok)
            b.dump("xn", b.xn, tok)
        if "gate1" in PHASES:
            ctok = b.gate(xn_cj)
            b.dump("c_em", b.c_em, ctok)
        if "ffn1" in PHASES:
            tile_tok, tok = b.ffn(xn_cj, ctok, eh_base)
            eh_base += 16
            b.dump("x_fm", b.x_fm, tok)
        if "ln2" in PHASES:
            dep2 = (lambda c, j: lntok) if COARSE_DEPS else (lambda c, j: tile_tok[c][j])
            lntok = tok
            xn_cj, tok = b.layernorm(dep2)
            xn_cj = coarsen(xn_cj, tok)
            b.dump("xn", b.xn, tok)
        if "attn" in PHASES:
            tile_tok, tok = b.attention(xn_cj, tok)
            b.h_free = tok
            b.sq_free = tok
            b.xcast_free = tok
            b.dump("x_fm", b.x_fm, tok)
        if "ln3" in PHASES:
            dep3 = (lambda c, j: lntok3) if COARSE_DEPS else (lambda c, j: tile_tok[c][j])
            lntok3 = tok
            xn_cj, tok = b.layernorm(dep3)
            xn_cj = coarsen(xn_cj, tok)
        if "gate2" in PHASES:
            ctok = b.gate(xn_cj)
        if "ffn2" in PHASES:
            tile_tok, tok = b.ffn(xn_cj, ctok, eh_base)
            b.epilogue([tile_tok[c][NJ - 1] for c in range(NCH)])
        else:
            b.epilogue([tok] * NCH)
        sems = {name: es.enter_context(nc.semaphore(name)) for name in b.p.counts}
        b.p.build(nc, sems)
    return nc


def _host_layout(L0_attn_x, L0_attn_W_q, L0_attn_W_k, L0_attn_gain, L0_attn_bias,
                 L0_attn_log_mix, L0_attn_log_scale, L0_ffn_gate_w, L0_ffn_w1,
                 L0_ffn_w2, L0_ffn_w3):
    x = np.asarray(L0_attn_x, dtype=np.float32)
    w1 = np.asarray(L0_ffn_w1, dtype=np.float32)
    w2 = np.asarray(L0_ffn_w2, dtype=np.float32)
    w3 = np.asarray(L0_ffn_w3, dtype=np.float32)
    wq = np.asarray(L0_attn_W_q, dtype=np.float32)
    wk = np.asarray(L0_attn_W_k, dtype=np.float32)
    gw = np.asarray(L0_ffn_gate_w, dtype=np.float32)
    gain = np.asarray(L0_attn_gain, dtype=np.float32)
    bias = np.asarray(L0_attn_bias, dtype=np.float32)
    lm = np.ascontiguousarray(np.asarray(L0_attn_log_mix, dtype=np.float32).reshape(1, 1))
    ls = np.ascontiguousarray(np.asarray(L0_attn_log_scale, dtype=np.float32).reshape(1, 1))

    xh = [np.ascontiguousarray(x[i].T.reshape(NCH, 128, T).transpose(1, 0, 2))
          for i in range(8)]

    def wlay13(w):
        t = w.reshape(NE, 2, NK, 128, NCH, 128)
        return np.ascontiguousarray(t.transpose(0, 1, 2, 5, 4, 3))

    w1h = wlay13(w1)
    w3h = wlay13(w3)
    t = w2.reshape(NE, 4, 256, 2, NK, 128)
    w2h = np.ascontiguousarray(t.transpose(0, 3, 1, 5, 4, 2))
    wqh = np.ascontiguousarray(wq.T.reshape(NCH, 128, DH).transpose(1, 0, 2))
    wkh = np.ascontiguousarray(wk.T.reshape(NCH, 128, DH).transpose(1, 0, 2))
    gwh = np.ascontiguousarray(gw.T.reshape(NCH, 128, NE).transpose(1, 0, 2))
    gbh = np.ascontiguousarray(
        np.concatenate([gain.reshape(NCH, 128).T, bias.reshape(NCH, 128).T], axis=1))
    shared = {"w1": w1h, "w3": w3h, "w2": w2h, "wq": wqh, "wk": wkh,
              "gw": gwh, "gb": gbh, "lm": lm, "ls": ls}
    return [dict(shared, x=xh[i]) for i in range(8)]


def kernel(**inputs):
    in_maps = _host_layout(**inputs)
    nc = build_program()
    res = run_bass_kernel_spmd(nc, in_maps, core_ids=list(range(8)))
    out = np.stack([
        np.ascontiguousarray(r["y"].transpose(1, 0, 2).reshape(D, T).T)
        for r in res.results])
    return out


# version 17 (final)
# speedup vs baseline: 1.0352x; 1.0352x over previous
import contextlib
import math
import numpy as np

import concourse.bass as bass
import concourse.mybir as mybir
from concourse.bass_utils import run_bass_kernel_spmd

F32 = mybir.dt.float32
BF16 = mybir.dt.bfloat16
U8 = mybir.dt.uint8
AF = mybir.ActivationFunctionType
ALU = mybir.AluOpType
AX = mybir.AxisListType

T = 2048
D = 1024
NCH = 8
NE = 8
NK = 8
NJ = 4
TW = 512
DH = 32
RSQ_DH = 1.0 / math.sqrt(32.0)
EPS = 1e-5
NEGBIG = -1.0e30
THRFLOOR = -1.0e29
NEGMASK = -1.0e9

PHASES = ("ln1", "gate1", "ffn1", "ln2", "attn", "ln3", "gate2", "ffn2")
DUMP = ()
DEBUG_GATE = False
ATTN_SB = 4
COARSE_DEPS = False
ATTN_MSG = True

ENGINES = ("tensor", "vector", "scalar", "sync", "gpsimd")


class Prog:

    def __init__(self):
        self.steps = []
        self.counts = {}
        self.dma_sems = set()

    def sem(self, name, dma=False):
        self.counts[name] = 0
        if dma:
            self.dma_sems.add(name)

    def emit(self, engine, fn, waits=(), inc=None):
        w = [t for t in waits if t is not None]
        tok = None
        if inc is not None:
            n = 16 if inc in self.dma_sems else 1
            self.counts[inc] += n
            tok = (inc, self.counts[inc])
        self.steps.append((engine, fn, w, inc, tok))
        return tok

    def build(self, nc, sems):
        with nc.Block() as block:
            for eng_name in ENGINES:
                def stream(eng, eng_name=eng_name):
                    own = {s: 0 for s in self.counts}
                    waited = {s: 0 for s in self.counts}
                    skip_own = False
                    for engine, fn, waits, inc, tok in self.steps:
                        if engine != eng_name:
                            continue
                        for s, v in waits:
                            if skip_own and s not in self.dma_sems and own[s] >= v:
                                continue
                            if waited[s] >= v:
                                continue
                            eng.wait_ge(sems[s], v)
                            waited[s] = v
                        instr = fn(eng)
                        if inc is not None:
                            instr.then_inc(sems[inc], 16 if inc in self.dma_sems else 1)
                            own[inc] = tok[1]
                getattr(block, eng_name)(stream)


class Builder:
    def __init__(self, nc, es):
        self.nc = nc
        self.es = es
        self.p = Prog()
        for s in ("PE", "VE", "SC", "GP"):
            self.p.sem(s)
        for s in ("ST", "Xgb", "Xlm", "Xls", "Wq", "Wk", "Wg",
                  "W1a", "W1b", "W3a", "W3b", "W2a", "W2b"):
            self.p.sem(s, dma=True)
        for c in range(NCH):
            self.p.sem(f"Xx{c}", dma=True)
        self._declare_dram()
        self._declare_sbuf()
        self._declare_psum()
        self.bank_free = [None] * 8
        self.w1_free = [None, None]
        self.w3_free = [None, None]
        self.w2_free = [None, None]
        self.s1_free = [None] * NJ
        self.x3_free = [None] * NJ
        self.slot_free = [None, None]
        self.h_free = None
        self.crep_free = None
        self.sq_free = None
        self.sim_free = None
        self.xcast_free = None
        self.attnT_free = None
        self.btmp_free = None
        self.gelo_free = None

    def _declare_dram(self):
        nc = self.nc
        dt = nc.dram_tensor
        self.d_x = dt("x", [128, NCH, T], F32, kind="ExternalInput").ap()
        self.d_w1 = dt("w1", [NE, 2, NK, 128, NCH, 128], F32, kind="ExternalInput").ap()
        self.d_w3 = dt("w3", [NE, 2, NK, 128, NCH, 128], F32, kind="ExternalInput").ap()
        self.d_w2 = dt("w2", [NE, 2, 4, 128, NK, 256], F32, kind="ExternalInput").ap()
        self.d_wq = dt("wq", [128, NCH, DH], F32, kind="ExternalInput").ap()
        self.d_wk = dt("wk", [128, NCH, DH], F32, kind="ExternalInput").ap()
        self.d_gw = dt("gw", [128, NCH, NE], F32, kind="ExternalInput").ap()
        self.d_gb = dt("gb", [128, 16], F32, kind="ExternalInput").ap()
        self.d_lm = dt("lm", [1, 1], F32, kind="ExternalInput").ap()
        self.d_ls = dt("ls", [1, 1], F32, kind="ExternalInput").ap()
        self.d_y = dt("y", [128, NCH, T], F32, kind="ExternalOutput").ap()
        self.d_dump = {}
        for name in DUMP:
            shp, dtp = {
                "xn": ([128, NCH * T], BF16), "c_em": ([8, T], F32),
                "x_fm": ([128, NCH * T], F32), "simb": ([128, T], F32),
                "scorem": ([128, T], F32), "attnb": ([128, T], BF16),
                "h_half": ([128, NK * T], BF16), "q": ([DH, T], BF16),
                "k": ([DH, T], BF16), "stats": ([4, T], F32),
                "xn_tm": ([128, 16 * 1024], BF16), "gsc_sb": ([8, T], F32),
                "gsc_tm": ([128, 128], F32), "g8": ([128, 8], F32),
                "gsc4": ([128, 4], F32), "ctm": ([128, 8], F32),
                "top8d": ([128, 8], F32),
            }[name]
            self.d_dump[name] = dt("dump_" + name, shp, dtp, kind="ExternalOutput").ap()

    def _declare_sbuf(self):
        nc, E = self.nc, self.es.enter_context
        self.x_fm = E(nc.sbuf_tensor("x_fm", [128, NCH * T], F32)).ap()
        self.xn = E(nc.sbuf_tensor("xn", [128, NCH * T], BF16)).ap()
        self.w1p = E(nc.sbuf_tensor("w1p", [128, 2 * 1024], BF16)).ap()
        self.w3p = E(nc.sbuf_tensor("w3p", [128, 2 * 1024], BF16)).ap()
        self.w2p = E(nc.sbuf_tensor("w2p", [128, 2 * 2048], BF16)).ap()
        self.gb = E(nc.sbuf_tensor("gbt", [128, 16], F32)).ap()
        self.wqt = E(nc.sbuf_tensor("wqt", [128, NCH * DH], BF16)).ap()
        self.wkt = E(nc.sbuf_tensor("wkt", [128, NCH * DH], BF16)).ap()
        self.gwt = E(nc.sbuf_tensor("gwt", [128, NCH * NE], BF16)).ap()
        self.tri = E(nc.sbuf_tensor("tri", [128, 128], F32)).ap()
        self.identf = E(nc.sbuf_tensor("identf", [128, 128], F32)).ap()
        self.identb = E(nc.sbuf_tensor("identb", [128, 128], BF16)).ap()
        self.ones_f = E(nc.sbuf_tensor("ones_f", [128, 1], F32)).ap()
        self.ones_b = E(nc.sbuf_tensor("ones_b", [128, 1], BF16)).ap()
        self.ones_k1 = E(nc.sbuf_tensor("ones_k1", [1, 128], F32)).ap()
        self.ones_k1b = E(nc.sbuf_tensor("ones_k1b", [1, 128], BF16)).ap()
        self.negb = E(nc.sbuf_tensor("negb", [128, 1], F32)).ap()
        self.scal = E(nc.sbuf_tensor("scal", [128, 4], F32)).ap()
        self.scbuf = E(nc.sbuf_tensor("scbuf", [1, 4], F32)).ap()
        self.lmls = E(nc.sbuf_tensor("lmls", [1, 2], F32)).ap()
        self.top8 = E(nc.sbuf_tensor("top8", [128, 16], F32)).ap()
        self.thr = E(nc.sbuf_tensor("thr", [128, 2], F32)).ap()
        self.negmax = E(nc.sbuf_tensor("negmax", [128, 2], F32)).ap()
        self.rsum = E(nc.sbuf_tensor("rsum", [128, 2], F32)).ap()
        self.rinv = E(nc.sbuf_tensor("rinv", [128, 2], F32)).ap()
        self.g8 = E(nc.sbuf_tensor("g8", [128, 8], F32)).ap()
        self.cvals = E(nc.sbuf_tensor("cvals", [128, 4], F32)).ap()
        self.gsc = E(nc.sbuf_tensor("gsc", [128, 4], F32)).ap()
        self.sel = E(nc.sbuf_tensor("sel", [8, 1024], BF16)).ap()
        AR = 90 * 1024
        self.arena = E(nc.sbuf_tensor("arena", [128, AR], U8)).ap()
        a = self.arena

        def vf32(off, n):
            return a[:, off:off + 4 * n].bitcast(F32)

        def vbf(off, n):
            return a[:, off:off + 2 * n].bitcast(BF16)

        self.xn_tm = vbf(0, 16 * 1024)
        self.attnT = vbf(32768, 16 * TW)
        self.simb = vf32(49152, T)
        self.simb2 = vf32(83968, T)
        self.scorem = vf32(57344, T)
        self.attnb = vbf(65536, T)
        self.mask8 = a[:, 69632:69632 + T]
        self.q_sb = vbf(71680, T)[0:DH, :]
        self.k_sb = vbf(75776, T)[0:DH, :]
        self.btmp = vf32(79872, TW)
        self.gelo = vf32(81920, TW)
        self.c_em = vbf(0, T)[0:8, :]
        self.c_rep = vf32(8192, T)
        self.gsc_sb = vf32(16384, T)[0:8, :]
        self.gsc_tm = vf32(24576, 128)
        self.ctm = vf32(25088, 128)
        self.g8w = vf32(33792, 128)[:, 0:128]
        self.m12 = vf32(34304, 32)
        self.gsc16 = vf32(34432, 16)
        self.s1b = vbf(25600, 4 * TW)
        self.x3b = vbf(29696, 4 * TW)
        self.h_half = vbf(34816, NK * T)
        self.x2buf = vbf(0, T)
        self.rstd_sb = vf32(4096, T)
        self.xcast = [vbf(4096, T), vbf(8192, T)]
        self.st_mean = vf32(12288, T)[0:1, :]
        self.st_rstd = vf32(20480, T)[0:1, :]
        self.st_tmp = vf32(12288, T)[64:65, :]
        self.st_mean_b = vbf(83968, T)[0:1, :]
        self.st_rstd_b = vbf(88064, T)[0:1, :]

    def _declare_psum(self):
        nc, E = self.nc, self.es.enter_context
        self.ps = [E(nc.psum_tensor(f"ps{i}", [128, TW], F32)).ap() for i in range(8)]
        self._psTb = [b.bitcast(BF16) for b in self.ps]

    def psT(self, sl, banks=(6, 7)):
        bank = self._psTb[banks[sl % 2]]
        j = (sl // 2) % 4
        return bank[:, j * 128:(j + 1) * 128]

    def mm(self, out, lhsT, rhs, start, stop, waits=(), inc=None):
        return self.p.emit(
            "tensor",
            lambda t, out=out, lhsT=lhsT, rhs=rhs, start=start, stop=stop:
                t.matmul(out, lhsT, rhs, start=start, stop=stop),
            waits=waits, inc=inc)

    def tp(self, out, in_, ident, waits=(), inc=None):
        return self.p.emit(
            "tensor",
            lambda t, out=out, in_=in_, ident=ident: t.transpose(out, in_, ident),
            waits=waits, inc=inc)

    def prologue(self):
        p = self.p
        self.tok_gb = p.emit("sync", lambda s: s.dma_start(out=self.gb, in_=self.d_gb),
                             inc="Xgb")
        tok_lm = p.emit("sync", lambda s: s.dma_start(out=self.lmls[:, 0:1], in_=self.d_lm),
                        inc="Xlm")
        tok_ls = p.emit("sync", lambda s: s.dma_start(out=self.lmls[:, 1:2], in_=self.d_ls),
                        inc="Xls")
        self.tok_x = []
        for c in range(NCH):
            self.tok_x.append(p.emit(
                "sync" if c % 2 == 0 else "scalar",
                lambda s, c=c: s.dma_start(
                    out=self.x_fm[:, c * T:(c + 1) * T], in_=self.d_x[:, c]),
                inc=f"Xx{c}"))
        self.xcast_free = None
        self.tok_wq = p.emit(
            "gpsimd", lambda g: g.dma_start(out=self.wqt,
                                            in_=self.d_wq.rearrange("p c h -> p (c h)")),
            inc="Wq")
        self.tok_wk = p.emit(
            "gpsimd", lambda g: g.dma_start(out=self.wkt,
                                            in_=self.d_wk.rearrange("p c h -> p (c h)")),
            inc="Wk")
        self.tok_gw = p.emit(
            "gpsimd", lambda g: g.dma_start(out=self.gwt,
                                            in_=self.d_gw.rearrange("p c e -> p (c e)")),
            inc="Wg")
        p.emit("gpsimd", lambda g: g.memset(self.ones_f, 1.0), inc="GP")
        p.emit("gpsimd", lambda g: g.memset(self.ones_b, 1.0), inc="GP")
        p.emit("gpsimd", lambda g: g.memset(self.ones_k1, 1.0), inc="GP")
        p.emit("gpsimd", lambda g: g.memset(self.ones_k1b, 1.0), inc="GP")
        p.emit("gpsimd", lambda g: g.memset(self.negb, NEGMASK), inc="GP")
        tif = p.emit("gpsimd", lambda g: g.memset(self.identf, 1.0), inc="GP")
        p.emit("gpsimd", lambda g: g.affine_select(
            out=self.identf, in_=self.identf, pattern=[[-1, 128]],
            channel_multiplier=1, base=0, compare_op=ALU.is_equal, fill=0.0),
            waits=[tif], inc="GP")
        tib = p.emit("gpsimd", lambda g: g.memset(self.identb, 1.0), inc="GP")
        p.emit("gpsimd", lambda g: g.affine_select(
            out=self.identb, in_=self.identb, pattern=[[-1, 128]],
            channel_multiplier=1, base=0, compare_op=ALU.is_equal, fill=0.0),
            waits=[tib], inc="GP")
        p.emit("gpsimd", lambda g: g.memset(self.cvals[:, 0:1], EPS), inc="GP")
        p.emit("gpsimd", lambda g: g.memset(self.cvals[:, 1:2], -0.5), inc="GP")
        p.emit("gpsimd", lambda g: g.memset(self.cvals[:, 2:3], RSQ_DH), inc="GP")
        p.emit("gpsimd", lambda g: g.memset(self.cvals[:, 3:4], -1.0), inc="GP")
        ttr = p.emit("gpsimd", lambda g: g.memset(self.tri, 0.0), inc="GP")
        self.tok_consts = p.emit("gpsimd", lambda g: g.affine_select(
            out=self.tri, in_=self.tri, pattern=[[-1, 128]],
            channel_multiplier=1, base=-1, compare_op=ALU.is_ge, fill=NEGBIG),
            waits=[ttr], inc="GP")
        tz = p.emit("gpsimd", lambda g: g.memset(self.scbuf, 0.0), inc="GP")
        t1 = p.emit("scalar",
                    lambda s: s.activation(self.scbuf[:, 0:1], self.lmls[:, 0:1], AF.Sigmoid),
                    waits=[tok_lm, tz], inc="SC")
        t2 = p.emit("vector",
                    lambda v: v.tensor_scalar(self.scbuf[:, 1:2], self.scbuf[:, 0:1],
                                              -1.0, 1.0, ALU.mult, ALU.add),
                    waits=[t1], inc="VE")
        t3a = p.emit("scalar",
                     lambda s: s.activation(self.scbuf[:, 2:3], self.lmls[:, 1:2], AF.Exp),
                     waits=[tok_ls, tz], inc="SC")
        t3b = p.emit("vector",
                     lambda v: v.tensor_scalar(self.scbuf[:, 2:3], self.scbuf[:, 2:3],
                                               1.0, None, ALU.add),
                     waits=[t3a], inc="VE")
        t3 = p.emit("scalar",
                    lambda s: s.activation(self.scbuf[:, 2:3], self.scbuf[:, 2:3], AF.Ln),
                    waits=[t3b], inc="SC")
        t4 = p.emit("vector",
                    lambda v: v.tensor_scalar(self.scbuf[:, 2:3], self.scbuf[:, 2:3],
                                              0.01, None, ALU.add),
                    waits=[t3], inc="VE")
        t5 = self.mm(self.ps[6][:, 0:4], self.ones_k1, self.scbuf, True, True,
                     waits=[t2, t4, self.tok_consts], inc="PE")
        self.tok_scal = p.emit("vector", lambda v: v.tensor_copy(self.scal, self.ps[6][:, 0:4]),
                               waits=[t5], inc="VE")
        self.bank_free[6] = self.tok_scal

    def layernorm(self, dep):
        p, ps = self.p, self.ps
        sum_toks = [None] * NJ
        cast_free = [self.xcast_free, self.xcast_free]
        for c in range(NCH):
            xc = self.xcast[c % 2]
            tc_ = p.emit("vector",
                         lambda v, c=c, xc=xc: v.tensor_copy(
                             xc, self.x_fm[:, c * T:(c + 1) * T]),
                         waits=[dep(c, 0), dep(c, 1), dep(c, 2), dep(c, 3),
                                cast_free[c % 2]], inc="VE")
            for j in range(NJ):
                tok = self.mm(ps[j][0:1, :], self.ones_b,
                              xc[:, j * TW:(j + 1) * TW],
                              start=(c == 0), stop=(c == NCH - 1),
                              waits=[tc_, self.bank_free[j]], inc="PE")
                sum_toks[j] = tok
            cast_free[c % 2] = sum_toks[-1]
        self.xcast_free = sum_toks[-1]
        sq_tile_toks = [None] * NJ
        prev_sq_mm = None
        for c in range(NCH):
            tsq = p.emit("scalar",
                         lambda s, c=c: s.activation(self.x2buf,
                                                     self.x_fm[:, c * T:(c + 1) * T],
                                                     AF.Square),
                         waits=[dep(c, 0), dep(c, 1), dep(c, 2), dep(c, 3),
                                prev_sq_mm if c > 0 else self.sq_free], inc="SC")
            for j in range(NJ):
                tok = self.mm(ps[4 + j][0:1, :], self.ones_b,
                              self.x2buf[:, j * TW:(j + 1) * TW],
                              start=(c == 0), stop=(c == NCH - 1),
                              waits=[tsq, self.bank_free[4 + j]], inc="PE")
                sq_tile_toks[j] = tok
                prev_sq_mm = tok
        ep = []
        for j in range(NJ):
            sl = slice(j * TW, (j + 1) * TW)
            t1 = p.emit("vector",
                        lambda v, j=j, sl=sl: v.tensor_scalar(
                            self.st_mean[:, sl], ps[j][0:1, :], -1.0 / D, None, ALU.mult),
                        waits=[sum_toks[j]], inc="VE")
            t2 = p.emit("vector",
                        lambda v, sl=sl: v.tensor_tensor(
                            self.st_tmp[:, sl], self.st_mean[:, sl], self.st_mean[:, sl],
                            ALU.mult),
                        waits=[t1], inc="VE")
            t3 = p.emit("vector",
                        lambda v, j=j, sl=sl: v.scalar_tensor_tensor(
                            self.st_tmp[:, sl], ps[4 + j][0:1, :], 1.0 / D,
                            self.st_tmp[:, sl], ALU.mult, ALU.subtract),
                        waits=[t2, sq_tile_toks[j]], inc="VE")
            t4 = p.emit("scalar",
                        lambda s, sl=sl: s.activation(self.st_rstd[:, sl],
                                                      self.st_tmp[:, sl], AF.Ln,
                                                      bias=self.cvals[0:1, 0:1]),
                        waits=[t3, self.tok_consts], inc="SC")
            t5 = p.emit("scalar",
                        lambda s, sl=sl: s.activation(self.st_rstd[:, sl],
                                                      self.st_rstd[:, sl], AF.Exp,
                                                      scale=self.cvals[0:1, 1:2]),
                        waits=[t4], inc="SC")
            t6 = p.emit("vector",
                        lambda v, sl=sl: v.tensor_copy(self.st_mean_b[:, sl],
                                                       self.st_mean[:, sl]),
                        waits=[t3], inc="VE")
            t7 = p.emit("vector",
                        lambda v, sl=sl: v.tensor_copy(self.st_rstd_b[:, sl],
                                                       self.st_rstd[:, sl]),
                        waits=[t5], inc="VE")
            ep.append((t6, t3, t7))
        rep_r = []
        for j in range(NJ):
            sl = slice(j * TW, (j + 1) * TW)
            tr = self.mm(ps[4 + j][:, :], self.ones_k1b, self.st_rstd_b[:, sl], True, True,
                         waits=[ep[j][2], ep[j][1]], inc="PE")
            tre = p.emit("vector",
                         lambda v, j=j, sl=sl: v.tensor_copy(self.rstd_sb[:, sl],
                                                             ps[4 + j][:, :]),
                         waits=[tr, self.xcast_free], inc="VE")
            rep_r.append(tre)
            self.bank_free[4 + j] = tre
        last = None
        lastm = [None] * NJ
        xn_cj = [[None] * NJ for _ in range(NCH)]
        for c in range(NCH):
            for j in range(NJ):
                sl_x = slice(c * T + j * TW, c * T + (j + 1) * TW)
                sl_t = slice(j * TW, (j + 1) * TW)
                tcp = self.mm(ps[j][:, :], self.identf,
                              self.x_fm[:, sl_x], True, False,
                              waits=[dep(c, j), lastm[j], sum_toks[j]])
                tmm = self.mm(ps[j][:, :], self.ones_k1b, self.st_mean_b[:, sl_t],
                              False, True, waits=[ep[j][0]], inc="PE")
                t2 = p.emit("vector",
                            lambda v, sl_x=sl_x, sl_t=sl_t, j=j: v.tensor_tensor(
                                self.xn[:, sl_x], ps[j][:, :], self.rstd_sb[:, sl_t],
                                ALU.mult),
                            waits=[tmm, rep_r[j]], inc="VE")
                lastm[j] = t2
                xn_cj[c][j] = t2
                last = t2
        for j in range(NJ):
            self.bank_free[j] = lastm[j]
        self.sq_free = prev_sq_mm
        return xn_cj, last

    def gate(self, xn_cj):
        p, ps = self.p, self.ps
        tsel0 = p.emit("gpsimd", lambda g: g.memset(self.sel, 1.0),
                       waits=[getattr(self, "sel_free", None)], inc="GP")
        tok_sel = p.emit("gpsimd", lambda g: g.affine_select(
            out=self.sel, in_=self.sel, pattern=[[1, 8], [0, 128]],
            channel_multiplier=-1, base=0, compare_op=ALU.is_equal, fill=0.0),
            waits=[tsel0], inc="GP")
        self.tok_sel = tok_sel
        sc_toks = []
        for j in range(NJ):
            bank = j % 2
            tok = None
            for c in range(NCH):
                tok = self.mm(ps[bank][0:8, :],
                              self.gwt[:, c * NE:(c + 1) * NE],
                              self.xn[:, c * T + j * TW: c * T + (j + 1) * TW],
                              start=(c == 0), stop=(c == NCH - 1),
                              waits=[xn_cj[c][j], self.tok_gw, self.bank_free[bank]],
                              inc="PE" if c == NCH - 1 else None)
            ev = p.emit("vector",
                        lambda v, j=j, bank=bank: v.tensor_copy(
                            self.gsc_sb[:, j * TW:(j + 1) * TW], ps[bank][0:8, :]),
                        waits=[tok], inc="VE")
            self.bank_free[bank] = ev
            sc_toks.append(ev)
        last = None
        tm_toks = []
        for b in range(16):
            j = (b * 128) // TW
            bank = 2 + b % 2
            tt = self.tp(ps[bank][:, 0:8],
                         self.gsc_sb[:, b * 128:(b + 1) * 128], self.identf[0:8, 0:8],
                         waits=[sc_toks[j], self.bank_free[bank]], inc="PE")
            ev = p.emit("scalar",
                        lambda s, b=b, bank=bank: s.activation(
                            self.gsc_tm[:, b * 8:(b + 1) * 8], ps[bank][:, 0:8],
                            AF.Copy),
                        waits=[tt], inc="SC")
            self.bank_free[bank] = ev
            tm_toks.append(ev)
        m_toks = []
        prev_cp = None
        for b in range(16):
            t1 = p.emit("vector",
                        lambda v, b=b: v.max(self.g8, self.gsc_tm[:, b * 8:(b + 1) * 8]),
                        waits=[tm_toks[b], prev_cp], inc="VE")
            cp = p.emit("scalar",
                        lambda s, b=b: s.activation(self.m12[:, 2 * b:2 * b + 2],
                                                    self.g8[:, 0:2], AF.Copy),
                        waits=[t1], inc="SC")
            prev_cp = cp
            m_toks.append(cp)
        m1a = self.m12[:, 0:32:2].unsqueeze(2).broadcast_to((128, 16, 8))
        m2a = self.m12[:, 1:32:2].unsqueeze(2).broadcast_to((128, 16, 8))
        t_sub = p.emit("vector",
                       lambda v: v.tensor_tensor(self.ctm, self.gsc_tm, m1a,
                                                 ALU.subtract),
                       waits=m_toks, inc="VE")
        t_exp = p.emit("scalar",
                       lambda s: s.activation(self.ctm, self.ctm, AF.Exp),
                       waits=[t_sub], inc="SC")
        t_msk = p.emit("vector",
                       lambda v: v.tensor_tensor(self.g8w, self.gsc_tm, m2a, ALU.is_ge),
                       waits=m_toks + [t_sub], inc="VE")
        t_d1 = p.emit("vector",
                      lambda v: v.tensor_tensor(self.gsc16, self.m12[:, 1:32:2],
                                                self.m12[:, 0:32:2], ALU.subtract),
                      waits=m_toks, inc="VE")
        t_d2 = p.emit("scalar", lambda s: s.activation(self.gsc16, self.gsc16, AF.Exp),
                      waits=[t_d1], inc="SC")
        t_d3 = p.emit("vector",
                      lambda v: v.tensor_scalar(self.gsc16, self.gsc16, 1.0, None,
                                                ALU.add),
                      waits=[t_d2], inc="VE")
        t_d4 = p.emit("scalar", lambda s: s.activation(self.gsc16, self.gsc16, AF.Ln),
                      waits=[t_d3], inc="SC")
        t_d5 = p.emit("scalar",
                      lambda s: s.activation(self.gsc16, self.gsc16, AF.Exp,
                                             scale=self.cvals[:, 3:4]),
                      waits=[t_d4, self.tok_consts], inc="SC")
        t_d6 = p.emit("vector",
                      lambda v: v.tensor_scalar(self.gsc16, self.gsc16, 0.5, None,
                                                ALU.mult),
                      waits=[t_d5], inc="VE")
        cca = self.gsc16.unsqueeze(2).broadcast_to((128, 16, 8))
        t_c1 = p.emit("vector",
                      lambda v: v.tensor_tensor(self.ctm, self.ctm, self.g8w, ALU.mult),
                      waits=[t_exp, t_msk], inc="VE")
        t_c2 = p.emit("vector",
                      lambda v, cca=cca: v.tensor_tensor(self.ctm, self.ctm, cca,
                                                         ALU.mult),
                      waits=[t_c1, t_d6], inc="VE")
        for b in range(16):
            bank2 = 4 + b % 2
            tt2 = self.tp(ps[bank2][0:8, 0:128], self.ctm[:, b * 8:(b + 1) * 8],
                          self.identf,
                          waits=[t_c2, self.bank_free[bank2]], inc="PE")
            ev2 = p.emit("scalar",
                         lambda s, b=b, bank2=bank2: s.activation(
                             self.c_em[:, b * 128:(b + 1) * 128], ps[bank2][0:8, 0:128],
                             AF.Copy),
                         waits=[tt2], inc="SC")
            self.bank_free[bank2] = ev2
            last = ev2
        return last

    def ffn(self, xn_cj, c_tok, eh_base):
        p, ps = self.p, self.ps
        last_evac = None
        tile_final = [[None] * NJ for _ in range(NCH)]
        for e in range(NE):
            crep_toks = []
            for j in range(NJ):
                bank = j % 2
                tm = self.mm(ps[bank][:, :], self.sel[:, e * 128:(e + 1) * 128],
                             self.c_em[:, j * TW:(j + 1) * TW], True, True,
                             waits=[c_tok, self.tok_sel, self.bank_free[bank]], inc="PE")
                ev = p.emit("vector",
                            lambda v, j=j, bank=bank: v.tensor_copy(
                                self.c_rep[:, j * TW:(j + 1) * TW], ps[bank][:, :]),
                            waits=[tm, self.crep_free], inc="VE")
                self.bank_free[bank] = ev
                crep_toks.append(ev)
            crep_ready = crep_toks[-1]
            for half in range(2):
                eh = eh_base + e * 2 + half
                h_tile_last = [None] * NJ
                for k in range(NK):
                    slot = (eh * NK + k) % 2
                    w1sl = self.w1p[:, slot * 1024:(slot + 1) * 1024]
                    w3sl = self.w3p[:, slot * 1024:(slot + 1) * 1024]
                    tok_w1 = p.emit(
                        "gpsimd",
                        lambda g, e=e, half=half, k=k, w1sl=w1sl: g.dma_start(
                            out=w1sl, in_=self.d_w1[e, half, k].rearrange("p c j -> p (c j)")),
                        waits=[self.w1_free[slot]], inc="W1a" if slot == 0 else "W1b")
                    tok_w3 = p.emit(
                        "gpsimd",
                        lambda g, e=e, half=half, k=k, w3sl=w3sl: g.dma_start(
                            out=w3sl, in_=self.d_w3[e, half, k].rearrange("p c j -> p (c j)")),
                        waits=[self.w3_free[slot]], inc="W3a" if slot == 0 else "W3b")
                    x1_stop = [None] * NJ
                    x3_stop = [None] * NJ
                    lastmm1 = lastmm3 = None
                    g3 = self.bank_free[7]
                    for j in range(NJ):
                        for c in range(NCH):
                            lastmm1 = self.mm(
                                ps[j][:, :], w1sl[:, c * 128:(c + 1) * 128],
                                self.xn[:, c * T + j * TW: c * T + (j + 1) * TW],
                                start=(c == 0), stop=(c == NCH - 1),
                                waits=[xn_cj[c][j], tok_w1, self.bank_free[j]],
                                inc="PE" if c == NCH - 1 else None)
                        x1_stop[j] = lastmm1
                    for j in range(NJ):
                        for c in range(NCH):
                            lastmm3 = self.mm(
                                ps[4 + j][:, :], w3sl[:, c * 128:(c + 1) * 128],
                                self.xn[:, c * T + j * TW: c * T + (j + 1) * TW],
                                start=(c == 0), stop=(c == NCH - 1),
                                waits=[xn_cj[c][j], tok_w3, g3],
                                inc="PE" if c == NCH - 1 else None)
                        x3_stop[j] = lastmm3
                    for j in range(NJ):
                        s1sl = self.s1b[:, j * TW:(j + 1) * TW]
                        x3sl = self.x3b[:, j * TW:(j + 1) * TW]
                        ts1 = p.emit("scalar",
                                     lambda s, s1sl=s1sl, j=j: s.activation(
                                         s1sl, ps[j][:, :], AF.Silu),
                                     waits=[x1_stop[j], self.s1_free[j]], inc="SC")
                        tx3 = p.emit("vector",
                                     lambda v, x3sl=x3sl, j=j: v.tensor_tensor(
                                         x3sl, ps[4 + j][:, :],
                                         self.c_rep[:, j * TW:(j + 1) * TW], ALU.mult),
                                     waits=[x3_stop[j], crep_ready, self.x3_free[j]],
                                     inc="VE")
                        th = p.emit("vector",
                                    lambda v, k=k, j=j, s1sl=s1sl, x3sl=x3sl: v.tensor_tensor(
                                        self.h_half[:, k * T + j * TW: k * T + (j + 1) * TW],
                                        s1sl, x3sl, ALU.mult),
                                    waits=[ts1, tx3, self.h_free], inc="VE")
                        self.bank_free[j] = ts1
                        self.bank_free[4 + j] = tx3
                        self.s1_free[j] = th
                        self.x3_free[j] = th
                        h_tile_last[j] = th
                        self.tx3_last = tx3
                    self.w1_free[slot] = lastmm1
                    self.w3_free[slot] = lastmm3
                g = 0
                lastw2 = None
                for dp in range(4):
                    slot = (eh * 4 + dp) % 2
                    w2sl = self.w2p[:, slot * 2048:(slot + 1) * 2048]
                    tok_w2 = p.emit(
                        "gpsimd",
                        lambda g_, e=e, half=half, dp=dp, w2sl=w2sl: g_.dma_start(
                            out=w2sl, in_=self.d_w2[e, half, dp].rearrange("p k q -> p (k q)")),
                        waits=[self.w2_free[slot]], inc="W2a" if slot == 0 else "W2b")
                    for j in range(NJ):
                        for di in range(2):
                            d = dp * 2 + di
                            bank = 4 + g % 4
                            g += 1
                            tok = None
                            for k in range(NK):
                                tok = self.mm(
                                    ps[bank][:, :],
                                    w2sl[:, k * 256 + di * 128: k * 256 + (di + 1) * 128],
                                    self.h_half[:, k * T + j * TW: k * T + (j + 1) * TW],
                                    start=(k == 0), stop=(k == NK - 1),
                                    waits=[tok_w2, self.bank_free[bank], h_tile_last[j]],
                                    inc="PE" if k == NK - 1 else None)
                            lastw2 = tok
                            ev = p.emit(
                                "vector",
                                lambda v, d=d, j=j, bank=bank: v.tensor_tensor(
                                    self.x_fm[:, d * T + j * TW: d * T + (j + 1) * TW],
                                    self.x_fm[:, d * T + j * TW: d * T + (j + 1) * TW],
                                    ps[bank][:, :], ALU.add),
                                waits=[tok], inc="VE")
                            self.bank_free[bank] = ev
                            last_evac = ev
                            tile_final[d][j] = ev
                    self.w2_free[slot] = lastw2
                self.h_free = lastw2
            self.crep_free = self.tx3_last
        return tile_final, last_evac

    def attention(self, xn_cj, xn_tok):
        p, ps = self.p, self.ps
        tile_final = [[None] * NJ for _ in range(NCH)]
        qk_toks = []
        for which in range(2):
            wt = self.wqt if which == 0 else self.wkt
            dst_t = self.q_sb if which == 0 else self.k_sb
            scale = self.cvals[0:DH, 2:3] if which == 0 else 1.0
            for j in range(NJ):
                tok = None
                for c in range(NCH):
                    tok = self.mm(ps[0][0:DH, :], wt[:, c * DH:(c + 1) * DH],
                                  self.xn[:, c * T + j * TW: c * T + (j + 1) * TW],
                                  start=(c == 0), stop=(c == NCH - 1),
                                  waits=[xn_cj[c][j], self.tok_wq, self.tok_wk,
                                         self.bank_free[0]],
                                  inc="PE" if c == NCH - 1 else None)
                ev = p.emit("scalar",
                            lambda s, dst_t=dst_t, j=j, scale=scale: s.activation(
                                dst_t[:, j * TW:(j + 1) * TW], ps[0][0:DH, :],
                                AF.Copy, scale=scale),
                            waits=[tok], inc="SC")
                self.bank_free[0] = ev
                qk_toks.append(ev)
        qk_ready = qk_toks[-1]
        xn_tm_last = None
        i = 0
        for sc in range(16):
            for c in range(NCH):
                sl = i % 8
                i += 1
                tt = self.tp(self.psT(sl, (4, 1)),
                             self.xn[:, c * T + sc * 128:c * T + (sc + 1) * 128],
                             self.identb,
                             waits=[xn_cj[c][sc // 4], self.slot_free[sl % 2],
                                    self.bank_free[4], self.bank_free[1]],
                             inc="PE")
                ev = p.emit("scalar",
                            lambda s, sc=sc, c=c, sl=sl: s.activation(
                                self.xn_tm[:, sc * 1024 + c * 128: sc * 1024 + (c + 1) * 128],
                                self.psT(sl, (4, 1)), AF.Copy),
                            waits=[tt], inc="SC")
                self.slot_free[sl % 2] = ev
                xn_tm_last = ev
        self.bank_free[0] = xn_tm_last
        self.bank_free[1] = xn_tm_last
        self.bank_free[4] = xn_tm_last
        prev_exp = None
        prev_rinv = None
        prev_tp_last = None
        prev_tth = None
        prev_sel = None
        sim_free2 = [None, None]
        last_x = xn_tm_last
        st = {"prev_exp": prev_exp, "prev_rinv": [None, None],
              "prev_tp_last": prev_tp_last, "prev_tth": [None, None],
              "prev_mask": None, "tab": {}, "attnT_sc": {}, "attnT_ve": {},
              "last_x": last_x, "msg_last": None}
        sim_free2 = [None, None]

        def emit_chain(b):
            par = b % 2
            simb = self.simb if par == 0 else self.simb2
            jmax = b // 4
            W = T if b == 0 else (jmax + 1) * TW
            sim_ev = []
            for j in range(jmax + 1):
                bank = j % 2
                wj = (b % 4 + 1) * 128 if (j == jmax and b > 0) else TW
                tok = None
                for c in range(NCH):
                    tok = self.mm(ps[bank][:, 0:wj],
                                  self.xn[:, c * T + b * 128: c * T + (b + 1) * 128],
                                  self.xn[:, c * T + j * TW: c * T + j * TW + wj],
                                  start=(c == 0), stop=(c == NCH - 1),
                                  waits=[xn_cj[c][b // 4], xn_cj[c][j],
                                         self.bank_free[bank]],
                                  inc="PE" if c == NCH - 1 else None)
                ev = p.emit("scalar",
                            lambda s, j=j, bank=bank, simb=simb, wj=wj: s.activation(
                                simb[:, j * TW:j * TW + wj], ps[bank][:, 0:wj],
                                AF.Copy),
                            waits=[tok, sim_free2[par]], inc="SC")
                self.bank_free[bank] = ev
                sim_ev.append(ev)
            td = p.emit("vector",
                        lambda v, b=b, simb=simb: v.tensor_tensor(
                            simb[:, b * 128:(b + 1) * 128],
                            simb[:, b * 128:(b + 1) * 128], self.tri, ALU.add),
                        waits=[sim_ev[-1], self.tok_consts, sim_free2[par]],
                        inc="VE")
            tu = None
            wmask = T if b == 0 else (jmax + 1) * TW
            if (b + 1) * 128 < wmask:
                tu = p.emit("vector",
                            lambda v, b=b, wmask=wmask, simb=simb: v.memset(
                                simb[:, (b + 1) * 128: wmask], NEGBIG),
                            waits=[td], inc="VE")
            wmax = (b + 1) * 128
            t8s = self.top8[:, par * 8:par * 8 + 8]
            thr = self.thr[:, par:par + 1]
            nmx = self.negmax[:, par:par + 1]
            rsm = self.rsum[:, par:par + 1]
            rnv = self.rinv[:, par:par + 1]
            t8 = p.emit("vector",
                        lambda v, wmax=wmax, simb=simb, t8s=t8s: v.max(
                            t8s, simb[:, 0:wmax]),
                        waits=[td, tu, st["prev_tth"][par]] + sim_ev, inc="VE")
            p.emit("vector", lambda v: v.engine_nop())
            tth = p.emit("vector",
                         lambda v, t8s=t8s, thr=thr: v.tensor_scalar(
                             thr, t8s[:, 7:8], THRFLOOR, None, ALU.max),
                         waits=[t8], inc="VE")
            st["prev_tth"][par] = tth
            mask_b = self.mask8
            if b == 0:
                tmk = p.emit("vector",
                             lambda v, W=W, simb=simb, thr=thr, mask_b=mask_b:
                             v.tensor_single_scalar(
                                 mask_b[:, 0:W], simb[:, 0:W],
                                 thr[:, 0:1], ALU.is_ge),
                             waits=[tth, st["prev_mask"]], inc="VE")
            else:
                tmk = p.emit("vector",
                             lambda v, W=W, simb=simb, thr=thr, mask_b=mask_b:
                             v.tensor_single_scalar(
                                 mask_b[:, 0:W], simb[:, 0:W],
                                 thr[:, 0:1], ALU.is_lt),
                             waits=[tth, st["prev_mask"]], inc="VE")
            sim_free2[par] = tmk
            jmaxb = b // 4
            sel_toks = []
            for j in range(jmaxb + 1):
                bank = 4 + j % 2
                tsc = self.mm(ps[bank][:, :],
                              self.q_sb[:, b * 128:(b + 1) * 128],
                              self.k_sb[:, j * TW:(j + 1) * TW],
                              True, True,
                              waits=[qk_ready, self.bank_free[bank]], inc="PE")
                if b == 0:
                    tse = p.emit("vector",
                                 lambda v, j=j, bank=bank, mask_b=mask_b: v.select(
                                     self.scorem[:, j * TW:(j + 1) * TW],
                                     mask_b[:, j * TW:(j + 1) * TW],
                                     ps[bank][:, :],
                                     self.negb[:, 0:1].broadcast_to((128, TW)),
                                     add_drain=True),
                                 waits=[tmk, tsc, st["prev_exp"]], inc="VE")
                else:
                    tse = p.emit("vector",
                                 lambda v, j=j, bank=bank, mask_b=mask_b:
                                 v.scalar_tensor_tensor(
                                     self.scorem[:, j * TW:(j + 1) * TW],
                                     mask_b[:, j * TW:(j + 1) * TW], NEGMASK,
                                     ps[bank][:, :], ALU.mult, ALU.add),
                                 waits=[tmk, tsc, st["prev_exp"]], inc="VE")
                self.bank_free[bank] = tse
                sel_toks.append(tse)
            if b == 0:
                for j in range(1, 4):
                    tse = p.emit("vector",
                                 lambda v, j=j: v.memset(
                                     self.scorem[:, j * TW:(j + 1) * TW], NEGMASK),
                                 waits=[tmk, st["prev_exp"]], inc="VE")
                    sel_toks.append(tse)
            st["prev_mask"] = sel_toks[-1]
            trm = p.emit("vector",
                         lambda v, W=W, nmx=nmx: v.reduce_max(
                             nmx, self.scorem[:, 0:W], axis=AX.X, negate=True),
                         waits=sel_toks + [st["prev_exp"]], inc="VE")
            texp = p.emit("scalar",
                          lambda s, W=W, nmx=nmx, rsm=rsm: s.activation(
                              self.attnb[:, 0:W], self.scorem[:, 0:W], AF.Exp,
                              bias=nmx[:, 0:1], accum_out=rsm[:, 0:1]),
                          waits=[trm, st["prev_rinv"][par], st["prev_tp_last"]],
                          inc="SC")
            st["prev_exp"] = texp
            tra = p.emit("scalar",
                         lambda s, rnv=rnv, rsm=rsm: s.activation(rnv, rsm, AF.Ln),
                         waits=[texp], inc="SC")
            tri_ = p.emit("scalar",
                          lambda s, rnv=rnv: s.activation(rnv, rnv, AF.Exp,
                                                          scale=self.cvals[:, 3:4]),
                          waits=[tra], inc="SC")
            st["prev_rinv"][par] = tri_
            tab = p.emit("vector",
                         lambda v, W=W, rnv=rnv: v.tensor_scalar(
                             self.attnb[:, 0:W], self.attnb[:, 0:W],
                             rnv[:, 0:1], None, ALU.mult),
                         waits=[tri_, st["prev_tp_last"]], inc="VE")
            st["tab"][b] = tab

        def emit_tp(b):
            S = b // 4
            bb = b % 4
            jmax = b // 4
            W = T if b == 0 else (jmax + 1) * TW
            npref = W // 128
            sc_top = 16 if S == 0 else 4 * (S + 1)
            tab = st["tab"][b]
            for sc in range(sc_top):
                sl = sc % 8
                if sc < npref:
                    tt = self.tp(self.psT(sl),
                                 self.attnb[:, sc * 128:(sc + 1) * 128],
                                 self.identb,
                                 waits=[tab, self.slot_free[sl % 2]], inc="PE")
                    st["prev_tp_last"] = tt
                    ev = p.emit("scalar",
                                lambda s, sc=sc, bb=bb, sl=sl: s.activation(
                                    self.attnT[:, sc * TW + bb * 128:
                                               sc * TW + (bb + 1) * 128],
                                    self.psT(sl), AF.Copy),
                                waits=[tt, self.attnT_free], inc="SC")
                    self.slot_free[sl % 2] = ev
                    st["attnT_sc"][S] = ev
                else:
                    ev = p.emit("vector",
                                lambda v, sc=sc, bb=bb: v.memset(
                                    self.attnT[:, sc * TW + bb * 128:
                                               sc * TW + (bb + 1) * 128], 0.0),
                                waits=[self.attnT_free], inc="VE")
                    st["attnT_ve"][S] = ev

        def emit_msg(S):
            sc_top = 16 if S == 0 else 4 * (S + 1)
            for d in range(NCH):
                bank = 2 + d % 2
                tok = None
                for sc in range(sc_top):
                    tok = self.mm(ps[bank][:, :],
                                  self.xn_tm[:, sc * 1024 + d * 128:
                                             sc * 1024 + (d + 1) * 128],
                                  self.attnT[:, sc * TW:(sc + 1) * TW],
                                  start=(sc == 0), stop=(sc == sc_top - 1),
                                  waits=[st["attnT_sc"].get(S), st["attnT_ve"].get(S),
                                         xn_tm_last, self.bank_free[bank]],
                                  inc="PE" if sc == sc_top - 1 else None)
                t1 = p.emit("vector",
                            lambda v, d=d, S=S: v.tensor_scalar(
                                self.btmp, self.xn[:, d * T + S * TW: d * T + (S + 1) * TW],
                                self.scal[:, 0:1], None, ALU.mult),
                            waits=[self.tok_scal, self.btmp_free], inc="VE")
                t2 = p.emit("vector",
                            lambda v, bank=bank: v.scalar_tensor_tensor(
                                self.btmp, ps[bank][:, :], self.scal[:, 1:2],
                                self.btmp, ALU.mult, ALU.add),
                            waits=[tok, t1], inc="VE")
                self.bank_free[bank] = t2
                t3 = p.emit("scalar",
                            lambda s, d=d: s.activation(
                                self.gelo, self.btmp, AF.Gelu,
                                bias=self.gb[:, 8 + d:9 + d], scale=self.gb[:, d:d + 1]),
                            waits=[t2, self.gelo_free, self.tok_gb], inc="SC")
                self.btmp_free = t3
                t4 = p.emit("vector",
                            lambda v, d=d, S=S: v.scalar_tensor_tensor(
                                self.x_fm[:, d * T + S * TW: d * T + (S + 1) * TW],
                                self.gelo, self.scal[:, 2:3],
                                self.x_fm[:, d * T + S * TW: d * T + (S + 1) * TW],
                                ALU.mult, ALU.add),
                            waits=[t3], inc="VE")
                self.gelo_free = t4
                tile_final[d][S] = t4
                st["msg_last"] = t4
            self.attnT_free = st["msg_last"]
            st["last_x"] = st["msg_last"]

        for b in range(4 * ATTN_SB):
            emit_chain(b)
            if b % 4 == 0 and b > 0 and ATTN_MSG:
                emit_msg(b // 4 - 1)
            emit_tp(b)
        if ATTN_MSG:
            emit_msg(ATTN_SB - 1)
        last_x = st["last_x"]
        self.bank_free[6] = last_x
        self.bank_free[7] = last_x
        return tile_final, last_x

    def dump(self, name, src, wait):
        if name not in self.d_dump:
            return
        self.p.emit("sync",
                    lambda s, src=src, name=name: s.dma_start(
                        out=self.d_dump[name], in_=src),
                    waits=[wait], inc="ST")

    def epilogue(self, chunk_toks):
        p = self.p
        for c in range(NCH):
            p.emit("sync",
                   lambda s, c=c: s.dma_start(
                       out=self.d_y[:, c], in_=self.x_fm[:, c * T:(c + 1) * T]),
                   waits=[chunk_toks[c]], inc="ST")
        finals = [(s, self.p.counts[s]) for s in self.p.dma_sems
                  if self.p.counts[s] > 0]
        p.emit("sync", lambda s: s.nop(), waits=finals)


def build_program():
    nc = bass.Bass("TRN2", target_bir_lowering=False)
    with contextlib.ExitStack() as es:
        b = Builder(nc, es)
        b.prologue()
        tok = b.tok_x[-1]
        tile_tok = None
        ctok = None
        xn_cj = None
        eh_base = 0
        def coarsen(xn_cj, tok):
            if COARSE_DEPS:
                return [[tok] * NJ for _ in range(NCH)]
            return xn_cj
        if "ln1" in PHASES:
            xn_cj, tok = b.layernorm(lambda c, j: b.tok_x[c])
            xn_cj = coarsen(xn_cj, tok)
            b.dump("xn", b.xn, tok)
        if "gate1" in PHASES:
            ctok = b.gate(xn_cj)
            b.dump("c_em", b.c_em, ctok)
        if "ffn1" in PHASES:
            tile_tok, tok = b.ffn(xn_cj, ctok, eh_base)
            eh_base += 16
            b.dump("x_fm", b.x_fm, tok)
        if "ln2" in PHASES:
            dep2 = (lambda c, j: lntok) if COARSE_DEPS else (lambda c, j: tile_tok[c][j])
            lntok = tok
            xn_cj, tok = b.layernorm(dep2)
            xn_cj = coarsen(xn_cj, tok)
            b.dump("xn", b.xn, tok)
        if "attn" in PHASES:
            tile_tok, tok = b.attention(xn_cj, tok)
            b.h_free = tok
            b.sq_free = tok
            b.xcast_free = tok
            b.dump("x_fm", b.x_fm, tok)
        if "ln3" in PHASES:
            dep3 = (lambda c, j: lntok3) if COARSE_DEPS else (lambda c, j: tile_tok[c][j])
            lntok3 = tok
            xn_cj, tok = b.layernorm(dep3)
            xn_cj = coarsen(xn_cj, tok)
        if "gate2" in PHASES:
            ctok = b.gate(xn_cj)
        if "ffn2" in PHASES:
            tile_tok, tok = b.ffn(xn_cj, ctok, eh_base)
            b.epilogue([tile_tok[c][NJ - 1] for c in range(NCH)])
        else:
            b.epilogue([tok] * NCH)
        sems = {name: es.enter_context(nc.semaphore(name)) for name in b.p.counts}
        b.p.build(nc, sems)
    return nc


def _host_layout(L0_attn_x, L0_attn_W_q, L0_attn_W_k, L0_attn_gain, L0_attn_bias,
                 L0_attn_log_mix, L0_attn_log_scale, L0_ffn_gate_w, L0_ffn_w1,
                 L0_ffn_w2, L0_ffn_w3):
    x = np.asarray(L0_attn_x, dtype=np.float32)
    w1 = np.asarray(L0_ffn_w1, dtype=np.float32)
    w2 = np.asarray(L0_ffn_w2, dtype=np.float32)
    w3 = np.asarray(L0_ffn_w3, dtype=np.float32)
    wq = np.asarray(L0_attn_W_q, dtype=np.float32)
    wk = np.asarray(L0_attn_W_k, dtype=np.float32)
    gw = np.asarray(L0_ffn_gate_w, dtype=np.float32)
    gain = np.asarray(L0_attn_gain, dtype=np.float32)
    bias = np.asarray(L0_attn_bias, dtype=np.float32)
    lm = np.ascontiguousarray(np.asarray(L0_attn_log_mix, dtype=np.float32).reshape(1, 1))
    ls = np.ascontiguousarray(np.asarray(L0_attn_log_scale, dtype=np.float32).reshape(1, 1))

    xh = [np.ascontiguousarray(x[i].T.reshape(NCH, 128, T).transpose(1, 0, 2))
          for i in range(8)]

    def wlay13(w):
        t = w.reshape(NE, 2, NK, 128, NCH, 128)
        return np.ascontiguousarray(t.transpose(0, 1, 2, 5, 4, 3))

    w1h = wlay13(w1)
    w3h = wlay13(w3)
    t = w2.reshape(NE, 4, 256, 2, NK, 128)
    w2h = np.ascontiguousarray(t.transpose(0, 3, 1, 5, 4, 2))
    wqh = np.ascontiguousarray(wq.T.reshape(NCH, 128, DH).transpose(1, 0, 2))
    wkh = np.ascontiguousarray(wk.T.reshape(NCH, 128, DH).transpose(1, 0, 2))
    gwh = np.ascontiguousarray(gw.T.reshape(NCH, 128, NE).transpose(1, 0, 2))
    gbh = np.ascontiguousarray(
        np.concatenate([gain.reshape(NCH, 128).T, bias.reshape(NCH, 128).T], axis=1))
    shared = {"w1": w1h, "w3": w3h, "w2": w2h, "wq": wqh, "wk": wkh,
              "gw": gwh, "gb": gbh, "lm": lm, "ls": ls}
    return [dict(shared, x=xh[i]) for i in range(8)]


def kernel(**inputs):
    in_maps = _host_layout(**inputs)
    nc = build_program()
    res = run_bass_kernel_spmd(nc, in_maps, core_ids=list(range(8)))
    out = np.stack([
        np.ascontiguousarray(r["y"].transpose(1, 0, 2).reshape(D, T).T)
        for r in res.results])
    return out
